# version 21
# baseline (speedup 1.0000x reference)
import numpy as np
from contextlib import ExitStack

import concourse.bass as bass
import concourse.mybir as mybir
from concourse.bass_utils import run_bass_kernel_spmd

F32 = mybir.dt.float32
BF16 = mybir.dt.bfloat16
AF = mybir.ActivationFunctionType
ALU = mybir.AluOpType
AX = mybir.AxisListType

EXP_SCALE = float(np.float32(1.0) / np.sqrt(np.float32(128.0)))
EXP_SHIFT = -10.0
EPS_LN = 1e-5
EPS_RMS = float(np.finfo(np.float32).eps)
G = 64


class Ctr:
    def __init__(self, sem, step=1):
        self.h = sem
        self.n = 0
        self.step = step

    def inc(self, ins):
        ins.then_inc(self.h, self.step)
        self.n += self.step
        return self.n


def build_program(L=4096, D=2048, HID=8192):
    H = D // 128
    NT = L // 128
    NQC = L // 512
    NDT = D // 128
    NHC = HID // 128
    W2F = min(2048, HID)
    NW2C = HID // W2F
    NT1 = HID // 128
    NTT = NT1 + NDT * NW2C
    assert L % 512 == 0 and D % 128 == 0 and HID % 128 == 0 and NT % 4 == 0

    nc = bass.Bass("TRN2", target_bir_lowering=False)
    x_hbm = nc.dram_tensor("x", [L, D], F32, kind="ExternalInput").ap()
    wfc = nc.dram_tensor("wfc", [HID, D], F32, kind="ExternalInput").ap()
    wproj = nc.dram_tensor("wproj", [D, HID], F32, kind="ExternalInput").ap()
    out_hbm = nc.dram_tensor("out", [L, D], F32, kind="ExternalOutput").ap()
    w1t = nc.dram_tensor("w1t", [NHC, 128, D], BF16, kind="Internal").ap()
    w2t = nc.dram_tensor("w2t", [NDT, 128, HID], BF16, kind="Internal").ap()
    ys = nc.dram_tensor("ys", [H, L, 129], F32, kind="Internal").ap()

    ctx = ExitStack()
    with ctx:
        ec = ctx.enter_context

        id_bf = ec(nc.sbuf_tensor("id_bf", [128, 128], BF16))
        id_f = ec(nc.sbuf_tensor("id_f", [128, 128], F32))
        tri = ec(nc.sbuf_tensor("tri", [128, 128], BF16))
        mneg = ec(nc.sbuf_tensor("mneg", [128, 128], BF16))
        nbias = ec(nc.sbuf_tensor("nbias", [128, 1], F32))
        eps_ln = ec(nc.sbuf_tensor("eps_ln", [128, 1], F32))
        eps_rms = ec(nc.sbuf_tensor("eps_rms", [128, 1], F32))
        ones_bf = ec(nc.sbuf_tensor("ones_bf", [128, 1], BF16))
        ones_f = ec(nc.sbuf_tensor("ones_f", [1, 32], F32))

        sp = Ctr(ec(nc.semaphore("sp")))
        sa = Ctr(ec(nc.semaphore("sa")))
        sv = Ctr(ec(nc.semaphore("sv")))
        sg = Ctr(ec(nc.semaphore("sg")))
        dW = [Ctr(ec(nc.semaphore(f"dW{i}")), 16) for i in range(2)]
        dSCR = [Ctr(ec(nc.semaphore(f"dSCR{i}")), 16) for i in range(2)]
        dX = [Ctr(ec(nc.semaphore(f"dX{i}")), 16) for i in range(2)]
        dYS = [Ctr(ec(nc.semaphore(f"dYS{i}")), 16) for i in range(2)]
        dXY = Ctr(ec(nc.semaphore("dXY")), 16)
        dXY2 = Ctr(ec(nc.semaphore("dXY2")), 16)
        dW1 = [Ctr(ec(nc.semaphore(f"dW1_{i}")), 16) for i in range(8)]
        dW2 = [Ctr(ec(nc.semaphore(f"dW2_{i}")), 16) for i in range(4)]
        dOUT = Ctr(ec(nc.semaphore("dOUT")), 16)

        block = ec(nc.Block())

        chain = {"tensor": sp, "vector": sv, "scalar": sa, "gpsimd": sg}
        marks = {e: 0 for e in chain}

        def snap_marks():
            for e in chain:
                marks[e] = chain[e].n

        def barrier_waits(eng, me):
            for e, c in chain.items():
                if e != me and marks[e] > 0:
                    eng.wait_ge(c.h, marks[e])

        @block.gpsimd
        def _(gp):
            def st(ins):
                sg.inc(ins)
                gp.wait_ge(sg.h, sg.n)
            st(gp.memset(id_bf[:], 0.0))
            st(gp.affine_select(out=id_bf[:], in_=id_bf[:], compare_op=ALU.not_equal,
                                fill=1.0, base=0, pattern=[[-1, 128]], channel_multiplier=1))
            st(gp.memset(id_f[:], 0.0))
            st(gp.affine_select(out=id_f[:], in_=id_f[:], compare_op=ALU.not_equal,
                                fill=1.0, base=0, pattern=[[-1, 128]], channel_multiplier=1))
            st(gp.memset(tri[:], 0.0))
            st(gp.affine_select(out=tri[:], in_=tri[:], compare_op=ALU.is_gt,
                                fill=1.0, base=0, pattern=[[-1, 128]], channel_multiplier=1))
            st(gp.memset(mneg[:], -3000.0))
            st(gp.affine_select(out=mneg[:], in_=mneg[:], compare_op=ALU.is_gt,
                                fill=0.0, base=0, pattern=[[-1, 128]], channel_multiplier=1))
            sg.inc(gp.memset(nbias[:], EXP_SHIFT))
            sg.inc(gp.memset(eps_ln[:], EPS_LN))
            sg.inc(gp.memset(eps_rms[:], EPS_RMS))
            sg.inc(gp.memset(ones_bf[:], 1.0))
            sg.inc(gp.memset(ones_f[:], 1.0))

        INIT_N0 = sg.n

        t_tiles = []
        for i in range(NT1):
            t_tiles.append((wfc[i * 128:(i + 1) * 128, :], w1t[i, :, :], D))
        for dt in range(NDT):
            for c in range(NW2C):
                t_tiles.append((wproj[dt * 128:(dt + 1) * 128, c * W2F:(c + 1) * W2F],
                                w2t[dt, :, c * W2F:(c + 1) * W2F], W2F))
        assert len(t_tiles) == NTT

        s1 = ExitStack()
        with s1:
            e1 = s1.enter_context
            wtf = [e1(nc.sbuf_tensor(f"wtf{i}", [128, 2048], F32)) for i in range(2)]
            wbb = [e1(nc.sbuf_tensor(f"wbb{i}", [128, 2048], BF16)) for i in range(2)]
            tgs = [e1(nc.sbuf_tensor(f"tgs{i}", [128, 2048 // G], F32)) for i in range(2)]
            tsc = [e1(nc.sbuf_tensor(f"tsc{i}", [128, 2048 // G], BF16)) for i in range(2)]
            trc = [e1(nc.sbuf_tensor(f"trc{i}", [128, 2048 // G], F32)) for i in range(2)]
            trr = [e1(nc.sbuf_tensor(f"trr{i}", [128, 2048], BF16)) for i in range(2)]
            tg1 = [e1(nc.sbuf_tensor(f"tg1{i}", [128, 2048], BF16)) for i in range(2)]
            tg2 = [e1(nc.sbuf_tensor(f"tg2{i}", [128, 2048], BF16)) for i in range(2)]
            tqt = [e1(nc.sbuf_tensor(f"tqt{i}", [128, 2048], BF16)) for i in range(4)]
            tst = [e1(nc.sbuf_tensor(f"tst{i}", [128, 16, 128], BF16,
                                       is_transpose_dest=True)) for i in range(2)]
            xres = [e1(nc.sbuf_tensor(f"xres{i}", [128, NT, 129], BF16)) for i in range(2)]
            xT = [e1(nc.sbuf_tensor(f"xT{i}", [128, L], BF16)) for i in range(2)]
            expS = [e1(nc.sbuf_tensor(f"expS{i}", [128, 1024], BF16)) for i in range(2)]
            ystg = [e1(nc.sbuf_tensor(f"ystg{i}", [128, 4, 129], F32)) for i in range(2)]
            ps_tr = [e1(nc.psum_tensor(f"ps_tr{i}", [128, 512], BF16)) for i in range(2)]
            ps_s = [e1(nc.psum_tensor(f"ps_s{i}", [128, 1024], F32)) for i in range(2)]
            ps_y = [e1(nc.psum_tensor(f"ps_y{i}", [128, 258], F32)) for i in range(2)]

            @block.gpsimd
            def _(gp):
                sg.inc(gp.memset(xres[0][:, :, 128:129], 1.0))
                sg.inc(gp.memset(xres[1][:, :, 128:129], 1.0))

            INIT_N = sg.n

            assert NTT % 2 == 0
            n_pairs = NTT // 2
            head_pairs = [[] for _ in range(H)]
            for p_ in range(n_pairs):
                head_pairs[(p_ * H) // n_pairs].append(p_)

            t_copy_pos = {}
            t_done_pos = {}
            t_ev_end = {}
            t_pe_end = {}
            tr_evict_pos = []
            tr_pe_pos = []

            def t_dve_ops(i, F):
                b = i % 2
                b4 = i % 4
                ng = F // G
                return [
                    lambda v: v.tensor_copy(wbb[b][:, 0:F], wtf[b][:, 0:F]),
                    lambda v: v.tensor_reduce(
                        tgs[b].rearrange("p (g o) -> p g o", o=1)[:, 0:ng, :],
                        wbb[b].rearrange("p (g e) -> p g e", e=G)[:, 0:ng, :],
                        AX.X, ALU.add, apply_absolute_value=True),
                    lambda v: v.tensor_scalar(tsc[b][:, 0:ng], tgs[b][:, 0:ng],
                                              1.0 / G, 1e-8, ALU.mult, ALU.max),
                    lambda v: v.reciprocal(trc[b][:, 0:ng], tsc[b][:, 0:ng]),
                    lambda v: v.tensor_tensor(
                        trr[b].rearrange("p (g e) -> p g e", e=G)[:, 0:ng, :],
                        wbb[b].rearrange("p (g e) -> p g e", e=G)[:, 0:ng, :],
                        trc[b].rearrange("p (g o) -> p g o", o=1)[:, 0:ng, :]
                           .broadcast_to([128, ng, G]),
                        ALU.mult),
                    lambda v: v.tensor_scalar(tg1[b][:, 0:F], trr[b][:, 0:F],
                                              0.5, None, ALU.is_gt),
                    lambda v: v.tensor_scalar(tg2[b][:, 0:F], trr[b][:, 0:F],
                                              -0.5, None, ALU.is_lt),
                    lambda v: v.tensor_sub(tqt[b4][:, 0:F], tg1[b][:, 0:F],
                                           tg2[b][:, 0:F]),
                    lambda v: v.tensor_tensor(
                        tqt[b4].rearrange("p (g e) -> p g e", e=G)[:, 0:ng, :],
                        tqt[b4].rearrange("p (g e) -> p g e", e=G)[:, 0:ng, :],
                        tsc[b].rearrange("p (g o) -> p g o", o=1)[:, 0:ng, :]
                           .broadcast_to([128, ng, G]),
                        ALU.mult),
                ]

            def emit_t_chunk(v, p_):
                ia, ib = 2 * p_, 2 * p_ + 1
                opsa = t_dve_ops(ia, t_tiles[ia][2])
                opsb = t_dve_ops(ib, t_tiles[ib][2])
                v.wait_ge(dW[0].h, 16 * (p_ + 1))
                v.wait_ge(dW[1].h, 16 * (p_ + 1))
                for k in range(9):
                    for i, ops in ((ia, opsa), (ib, opsb)):
                        if k > 0:
                            v.wait_ge(sv.h, sv.n - 1)
                        if k == 7 and i >= 4:
                            v.wait_ge(sp.h, t_pe_end[i - 4])
                        sv.inc(ops[k](v))
                        if k == 0:
                            assert sv.n == t_copy_pos[i]
                        if k == 8:
                            assert sv.n == t_done_pos[i]

            def emit_t_sync_loads(sy, p_):
                for i in (2 * p_, 2 * p_ + 1):
                    if i >= 2:
                        sy.wait_ge(sv.h, t_copy_pos[i - 2])
                    dW[i % 2].inc(sy.dma_start(out=wtf[i % 2][:, 0:t_tiles[i][2]],
                                               in_=t_tiles[i][0]))

            def emit_t_scratch(q, p_):
                for i in (2 * p_, 2 * p_ + 1):
                    F = t_tiles[i][2]
                    q.wait_ge(sa.h, t_ev_end[i])
                    dSCR[i % 2].inc(q.dma_start(out=t_tiles[i][1],
                                                in_=tst[i % 2][:, 0:F // 128, :]))

            def head_x_src(h):
                full = x_hbm[:, h * 128:(h + 1) * 128].rearrange("(a p) d -> p a d", p=128)
                return [full[:, q * (NT // 4):(q + 1) * (NT // 4), :] for q in range(4)]

            @block.gpsimd
            def _(gp):
                barrier_waits(gp, "gpsimd")
                for h in range(min(2, H)):
                    for q, src in enumerate(head_x_src(h)):
                        dX[h % 2].inc(gp.dma_start(
                            out=xres[h % 2][:, q * (NT // 4):(q + 1) * (NT // 4), 0:128],
                            in_=src))

            exp_abs = {}
            pe_head_end = []
            gks = 0

            xt_ready = {}
            prev_yev_end = 0
            exp_abs_g = {}
            ggs = 0

            for h in range(H):
                pe_mark_h = sp.n
                act_mark_h = sa.n
                dve_mark_h = sv.n
                gtr0_h = len(tr_pe_pos)

                hp = head_pairs[h]
                qs = {}
                for idx, p_ in enumerate(hp):
                    if len(hp) <= max(NQC - 2, 1):
                        qc_ = min(2, NQC - len(hp)) + idx
                    else:
                        qc_ = idx // 2
                    qs[p_] = min(qc_, NQC - 1)
                slot_chunks = {qc: [] for qc in range(NQC)}
                for p_ in hp:
                    slot_chunks[qs[p_]].append(p_)
                pairs_by_region = {qc: [] for qc in range(NQC + 1)}
                for p_ in hp:
                    pairs_by_region[min(qs[p_] + 1, NQC)].append(p_)
                region_jobs = {qc: [] for qc in range(NQC + 1)}
                for p_ in hp:
                    reg = min(qs[p_] + 2, NQC)
                    for i in (2 * p_, 2 * p_ + 1):
                        nj = t_tiles[i][2] // 128
                        for j in range(0, nj, 4):
                            region_jobs[reg].append(("t", i, j, min(4, nj - j)))
                jobs_x = ([("x", h + 1, j, min(4, NT - j))
                           for j in range(0, NT, 4)] if h + 1 < H else [])
                wts = [5 * (4 * qc + 4) - 6 for qc in range(NQC)]
                tot = sum(wts)
                acc = 0
                taken = 0
                for qc in range(NQC):
                    acc += wts[qc]
                    upto = (len(jobs_x) * acc) // tot
                    region_jobs[qc] = jobs_x[taken:upto] + region_jobs[qc]
                    taken = upto
                prologue_jobs = ([("x", 0, j, min(4, NT - j))
                                  for j in range(0, NT, 4)] if h == 0 else [])

                qc_groups = []
                for qc in range(NQC):
                    gl = []
                    if qc >= 1:
                        gl += [[2 * i, 2 * i + 1] for i in range(2 * qc)]
                    gl += [[4 * qc + t0] for t0 in range(4)]
                    qc_groups.append(gl)

                pos = 0
                for j_ in prologue_jobs:
                    pos += 1
                    tr_pe_pos.append(pe_mark_h + pos)
                qc_plan = []
                for qc in range(NQC):
                    nk = 4 * qc + 4
                    groups = qc_groups[qc]
                    ng = len(groups)
                    pv_lists = [list(range(max(0, kt - 4 * qc), 4)) for kt in range(nk)]
                    jobs = region_jobs[qc]
                    share = [0] * ng
                    for x in range(len(jobs)):
                        share[(x * ng) // len(jobs)] += 1
                    seq = [("s", 0), ("s", 1)]
                    ji = 0
                    for gi in range(ng):
                        seq.append(("pv", gi))
                        for _ in range(share[gi]):
                            seq.append(("tr", jobs[ji])); ji += 1
                        if gi + 2 < ng:
                            seq.append(("s", gi + 2))
                    assert ji == len(jobs)
                    s_end = [0] * ng
                    pv_end_g = [0] * ng
                    for op in seq:
                        if op[0] == "s":
                            pos += 1
                            s_end[op[1]] = pos
                        elif op[0] == "pv":
                            group = groups[op[1]]
                            if len(group) == 1 and (group[0] - 4 * qc) % 2 == 1:
                                pos += 1
                            pv_end_g[op[1]] = pos
                        else:
                            pos += 1
                            tr_pe_pos.append(pe_mark_h + pos)
                            kind, a1, a2, n_ = op[1]
                            if kind == "t" and a2 + n_ == t_tiles[a1][2] // 128:
                                t_pe_end[a1] = pe_mark_h + pos
                    qc_plan.append((groups, seq, s_end, pv_end_g, pv_lists))
                    pos = pos
                for op1 in region_jobs[NQC]:
                    pos += 1
                    tr_pe_pos.append(pe_mark_h + pos)
                    kind, a1, a2, n_ = op1
                    if kind == "t" and a2 + n_ == t_tiles[a1][2] // 128:
                        t_pe_end[a1] = pe_mark_h + pos

                a = 0
                for j_ in prologue_jobs:
                    a += 1
                    tr_evict_pos.append(act_mark_h + a)
                if h == 0 and prologue_jobs:
                    xt_ready[0] = act_mark_h + a
                ks = gks
                gg = ggs
                for qc in range(NQC):
                    groups = qc_groups[qc]
                    for op in qc_plan[qc][1]:
                        if op[0] == "s":
                            gi = op[1]
                            group = groups[gi]
                            a += 1
                            for kt in group:
                                exp_abs[ks + kt] = act_mark_h + a
                            exp_abs_g[gg + gi] = act_mark_h + a
                        elif op[0] == "tr":
                            a += 1
                            tr_evict_pos.append(act_mark_h + a)
                            kind, a1, a2, n_ = op[1]
                            if kind == "x" and a2 + n_ == NT:
                                xt_ready[a1] = act_mark_h + a
                            if kind == "t" and a2 + n_ == t_tiles[a1][2] // 128:
                                t_ev_end[a1] = act_mark_h + a
                    gg += len(groups)
                    ks += 4 * qc + 4
                for op1 in region_jobs[NQC]:
                    a += 1
                    tr_evict_pos.append(act_mark_h + a)
                    kind, a1, a2, n_ = op1
                    if kind == "x" and a2 + n_ == NT:
                        xt_ready[a1] = act_mark_h + a
                    if kind == "t" and a2 + n_ == t_tiles[a1][2] // 128:
                        t_ev_end[a1] = act_mark_h + a
                n_groups_h = gg - ggs

                dpos = dve_mark_h
                evict_pos = {}
                for qc in range(NQC):
                    for j in range(2):
                        dpos += 1
                        evict_pos[(qc, j)] = dpos
                    for p_ in slot_chunks[qc]:
                        for k in range(9):
                            for i in (2 * p_, 2 * p_ + 1):
                                dpos += 1
                                if k == 0:
                                    t_copy_pos[i] = dpos
                                if k == 8:
                                    t_done_pos[i] = dpos

                @block.tensor
                def _(t, h=h, qc_plan=qc_plan, pe_mark_h=pe_mark_h, gks=gks,
                      ggs=ggs, region_jobs=region_jobs,
                      prologue_jobs=prologue_jobs, gtr0_h=gtr0_h,
                      prev_yev_end=prev_yev_end, evict_pos=evict_pos):
                    if h == 0:
                        barrier_waits(t, "tensor")
                        t.wait_ge(dX[0].h, 16 * 4)
                    g = gtr0_h
                    seen_x = set()
                    seen_t = set()

                    def emit_tr(op1, g):
                        kind, a1, a2, n_ = op1
                        if kind == "x":
                            if a1 not in seen_x:
                                t.wait_ge(dX[a1 % 2].h, 16 * 4 * (a1 // 2 + 1))
                                seen_x.add(a1)
                        else:
                            if a1 not in seen_t:
                                t.wait_ge(sv.h, t_done_pos[a1])
                                seen_t.add(a1)
                        if g >= 2:
                            t.wait_ge(sa.h, tr_evict_pos[g - 2])
                        for k in range(n_):
                            if kind == "x":
                                src = xres[a1 % 2][:, a2 + k, 0:128]
                            else:
                                src = tqt[a1 % 4][:, (a2 + k) * 128:(a2 + k + 1) * 128]
                            ins = t.transpose(ps_tr[g % 2][:, k * 128:(k + 1) * 128],
                                              src, id_bf[:])
                            if k == n_ - 1:
                                sp.inc(ins)
                        assert sp.n == tr_pe_pos[g]

                    for op1 in prologue_jobs:
                        emit_tr(op1, g); g += 1
                    ks = gks
                    gg = ggs
                    for qc in range(NQC):
                        groups, seq, s_end, pv_end_g, pv_lists = qc_plan[qc]

                        def emit_s(gi, ks=ks, qc=qc, gg=gg, groups=groups):
                            gx = gg + gi
                            if gx >= 2:
                                t.wait_ge(sa.h, exp_abs_g[gx - 2])
                            if gi == 0 and qc == 0:
                                t.wait_ge(sa.h, xt_ready[h])
                            diag = len(groups[gi]) == 1 and groups[gi][0] >= 4 * qc
                            for idx, kt in enumerate(groups[gi]):
                                qlo = 128 * max(0, kt - 4 * qc)
                                ins = t.matmul(
                                    ps_s[gx % 2][:, 512 * idx + qlo: 512 * idx + 512],
                                    xT[h % 2][:, kt * 128:(kt + 1) * 128],
                                    xT[h % 2][:, qc * 512 + qlo: qc * 512 + 512],
                                    start=True, stop=not diag)
                                if diag:
                                    ins = t.matmul(
                                        ps_s[gx % 2][:, qlo:qlo + 128],
                                        id_bf[:], mneg[:],
                                        start=False, stop=True)
                                if idx == len(groups[gi]) - 1:
                                    sp.inc(ins)

                        def emit_pv(gi, ks=ks, qc=qc, gg=gg, groups=groups,
                                    pv_lists=pv_lists):
                            gx = gg + gi
                            first = True
                            for idx, kt in enumerate(groups[gi]):
                                kk = ks + kt
                                for u in pv_lists[kt]:
                                    if first:
                                        t.wait_ge(sa.h, exp_abs[kk])
                                        if kt == 0 and qc >= 1:
                                            t.wait_ge(sv.h, evict_pos[(qc - 1, 1)])
                                        if kt == 0 and qc == 0 and prev_yev_end > 0:
                                            t.wait_ge(sv.h, prev_yev_end)
                                        first = False
                                    stop_ = (u % 2 == 1 and kt == 4 * qc + u)
                                    ins = t.matmul(
                                        ps_y[u // 2][:, (u % 2) * 129:(u % 2) * 129 + 129],
                                        expS[gx % 2][:, 512 * idx + u * 128: 512 * idx + (u + 1) * 128],
                                        xres[h % 2][:, kt, 0:129],
                                        start=(kt == 0 and u % 2 == 0),
                                        stop=stop_)
                                    if stop_:
                                        sp.inc(ins)

                        for op in seq:
                            if op[0] == "s":
                                emit_s(op[1])
                            elif op[0] == "pv":
                                emit_pv(op[1])
                            else:
                                emit_tr(op[1], g); g += 1
                        ks += 4 * qc + 4
                        gg += len(groups)
                    for op1 in region_jobs[NQC]:
                        emit_tr(op1, g); g += 1

                @block.scalar
                def _(s, h=h, qc_plan=qc_plan, pe_mark_h=pe_mark_h, gks=gks,
                      ggs=ggs, region_jobs=region_jobs,
                      prologue_jobs=prologue_jobs, gtr0_h=gtr0_h):
                    if h == 0:
                        barrier_waits(s, "scalar")
                    g = gtr0_h
                    seen_t = set()

                    def emit_cp(op1, g):
                        kind, a1, a2, n_ = op1
                        s.wait_ge(sp.h, tr_pe_pos[g])
                        if kind == "x":
                            sa.inc(s.copy(xT[a1 % 2][:, a2 * 128:(a2 + n_) * 128],
                                          ps_tr[g % 2][:, 0:n_ * 128]))
                        else:
                            if a1 not in seen_t and a1 >= 2:
                                s.wait_ge(dSCR[a1 % 2].h, 16 * ((a1 - 2) // 2 + 1))
                            seen_t.add(a1)
                            sa.inc(s.copy(tst[a1 % 2][:, a2:a2 + n_, :],
                                          ps_tr[g % 2][:, 0:n_ * 128]))
                        assert sa.n == tr_evict_pos[g]

                    for op1 in prologue_jobs:
                        emit_cp(op1, g); g += 1
                    ks = gks
                    gg = ggs
                    for qc in range(NQC):
                        groups, seq, s_end, pv_end_g, pv_lists = qc_plan[qc]
                        for op in seq:
                            if op[0] == "s":
                                gi = op[1]
                                group = groups[gi]
                                gx = gg + gi
                                s.wait_ge(sp.h, pe_mark_h + s_end[gi])
                                if len(group) == 2:
                                    sa.inc(s.activation(expS[gx % 2][:, 0:1024],
                                                        ps_s[gx % 2][:, 0:1024],
                                                        AF.Exp, bias=nbias[:],
                                                        scale=EXP_SCALE))
                                else:
                                    kt = group[0]
                                    qlo = 128 * max(0, kt - 4 * qc)
                                    sa.inc(s.activation(expS[gx % 2][:, qlo:512],
                                                        ps_s[gx % 2][:, qlo:512],
                                                        AF.Exp, bias=nbias[:],
                                                        scale=EXP_SCALE))
                                assert sa.n == exp_abs_g[gx]
                            elif op[0] == "tr":
                                emit_cp(op[1], g); g += 1
                        ks += 4 * qc + 4
                        gg += len(groups)
                    for op1 in region_jobs[NQC]:
                        emit_cp(op1, g); g += 1

                @block.vector
                def _(v, h=h, qc_plan=qc_plan, pe_mark_h=pe_mark_h,
                      evict_pos=evict_pos, slot_chunks=slot_chunks):
                    if h == 0:
                        barrier_waits(v, "vector")
                        v.wait_ge(sg.h, INIT_N)
                    for qc in range(NQC):
                        groups, seq, s_end, pv_end_g, pv_lists = qc_plan[qc]
                        for j in range(2):
                            gi_last = groups.index([4 * qc + 2 * j + 1])
                            v.wait_ge(sp.h, pe_mark_h + pv_end_g[gi_last])
                            if j == 0:
                                idx_par = (NQC // 2) * h + qc // 2
                                if idx_par >= 1:
                                    v.wait_ge(dYS[qc % 2].h, 16 * idx_par)
                            sv.inc(v.tensor_copy(
                                ystg[qc % 2][:, 2 * j:2 * j + 2, :],
                                ps_y[j].rearrange("p (a e) -> p a e", e=129)))
                            assert sv.n == evict_pos[(qc, j)]
                        for p_ in slot_chunks[qc]:
                            emit_t_chunk(v, p_)

                @block.gpsimd
                def _(gp, h=h, gks=gks, evict_pos=evict_pos):
                    for qc in range(NQC):
                        gp.wait_ge(sv.h, evict_pos[(qc, 1)])
                        dYS[qc % 2].inc(gp.dma_start(
                            out=ys[h].rearrange("(a p) e -> p a e", p=128)
                            [:, 4 * qc:4 * qc + 4, :],
                            in_=ystg[qc % 2][:, :, :]))

                @block.sync
                def _(sy, h=h):
                    if h == 0:
                        barrier_waits(sy, "sync")
                    hp_ = head_pairs[h]
                    for k2, p_ in enumerate(hp_):
                        emit_t_sync_loads(sy, p_)
                        if k2 >= 1:
                            emit_t_scratch(sy, hp_[k2 - 1])
                    if hp_:
                        emit_t_scratch(sy, hp_[-1])

                gks += sum(4 * qc + 4 for qc in range(NQC))
                ggs += n_groups_h
                pe_head_end.append(sp.n)
                prev_yev_end = evict_pos[(NQC - 1, 1)]

                if h + 2 < H:
                    @block.gpsimd
                    def _(gp, h=h):
                        gp.wait_ge(sp.h, pe_head_end[h])
                        for q, src in enumerate(head_x_src(h + 2)):
                            dX[h % 2].inc(gp.dma_start(
                                out=xres[h % 2][:, q * (NT // 4):(q + 1) * (NT // 4), 0:128],
                                in_=src))

            snap_marks()

        s2 = ExitStack()
        with s2:
            e2 = s2.enter_context
            xb = [e2(nc.sbuf_tensor(f"xb{i}", [128, 4, D], F32)) for i in range(2)]
            yb = e2(nc.sbuf_tensor("yb", [128, 2, H, 129], F32))
            rcp16 = e2(nc.sbuf_tensor("rcp16", [128, H], F32))
            X1T = [e2(nc.sbuf_tensor(f"X1T{i}", [128, NDT, 512], BF16)) for i in range(1)]
            hbuf = e2(nc.sbuf_tensor("hbuf", [128, NHC, 512], BF16))
            W1K = min(7, HID // 128)
            w1b = [e2(nc.sbuf_tensor(f"w1b{i}", [128, D], BF16))
                   for i in range(W1K)]
            w2b = [e2(nc.sbuf_tensor(f"w2b{i}", [128, HID // 4], BF16)) for i in range(2)]
            trel = [e2(nc.sbuf_tensor(f"trel{i}", [128, 512], BF16)) for i in range(2)]
            hsq = trel
            lnt = e2(nc.sbuf_tensor("lnt", [1, 512], F32))
            rsb4 = e2(nc.sbuf_tensor("rsb4", [128, 4], F32))
            yTd = [e2(nc.sbuf_tensor(f"yTd{i}", [128, 512], F32)) for i in range(2)]
            sqb = e2(nc.sbuf_tensor("sqb", [128, D // 4], F32))
            statv = e2(nc.sbuf_tensor("statv", [128, 8], F32))
            s1c = statv[:, 0:1]
            s2c = statv[:, 1:2]
            mc = statv[:, 2:3]
            m2c = statv[:, 3:4]
            vc = statv[:, 4:5]
            rsc = statv[:, 5:6]
            ps_h = [e2(nc.psum_tensor(f"ps_h{i}", [128, 512], F32)) for i in range(2)]
            ps_m = [e2(nc.psum_tensor(f"ps_m{i}", [128, 512], F32)) for i in range(2)]
            ps_den = e2(nc.psum_tensor("ps_den", [128, 512], F32))
            ps_bc = e2(nc.psum_tensor("ps_bc", [128, 512], F32))
            ps_tf = [e2(nc.psum_tensor(f"ps_tf{i}", [128, 128], F32)) for i in range(2)]

            trp_pe = []
            trp_ev = []
            LN_OPS = 17
            CHUNK_OPS = H + 1 + LN_OPS

            def emit_ln(v, st, xap, gate_apply):
                Q4 = D // 4
                st(v.tensor_reduce(s1c[:], xap, AX.X, ALU.add))
                st(v.tensor_scalar(mc[:], s1c[:], 1.0 / D, None, ALU.mult))
                for q4 in range(4):
                    xs = xap[:, q4 * Q4:(q4 + 1) * Q4]
                    st(v.tensor_tensor(sqb[:, :], xs, xs, ALU.mult))
                    st(v.tensor_reduce(m2c[:] if q4 else s2c[:], sqb[:, :],
                                       AX.X, ALU.add))
                    if q4:
                        st(v.tensor_add(s2c[:], s2c[:], m2c[:]))
                st(v.tensor_scalar(s2c[:], s2c[:], 1.0 / D, None, ALU.mult))
                st(v.tensor_tensor(m2c[:], mc[:], mc[:], ALU.mult))
                st(v.tensor_sub(vc[:], s2c[:], m2c[:]))
                gate_apply()
                st(v.tensor_scalar(xap, xap, mc[:], rsc[:],
                                   ALU.subtract, ALU.mult))

            pe_b, sa_b, sv_b, dxy_b = sp.n, sa.n, sv.n, dXY.n

            def bp0(tp, j, sv_b=sv_b):
                return sv_b + tp * CHUNK_OPS + j

            @block.sync
            def _(sy):
                barrier_waits(sy, "sync")
                sy.wait_ge(dYS[0].h, 16 * (NQC // 2) * H)
                sy.wait_ge(dYS[1].h, 16 * (NQC // 2) * H)
                sy.wait_ge(dSCR[0].h, 16 * ((NTT + 1) // 2))
                sy.wait_ge(dSCR[1].h, 16 * (NTT // 2))
                dXY.inc(sy.dma_start(
                    out=xb[0][:, :, :],
                    in_=x_hbm[0:512, :].rearrange("(a p) d -> p a d", p=128)))
                for tp in range(4):
                    if tp == 2:
                        sy.wait_ge(sv.h, bp0(1, H + 1))
                    for h in range(H):
                        dXY.inc(sy.dma_start(
                            out=yb[:, tp % 2, h, :],
                            in_=ys[h].rearrange("(a p) e -> p a e", p=128)[:, tp, :]))

            @block.vector
            def _(v):
                def st(ins):
                    sv.inc(ins)
                    v.wait_ge(sv.h, sv.n)
                barrier_waits(v, "vector")
                for tp in range(4):
                    v.wait_ge(dXY.h, dxy_b + 16 * (1 + (2 if tp < 2 else 4) * H))
                    st(v.reciprocal(rcp16[:, :], yb[:, tp % 2, :, 128]))
                    for h in range(H):
                        st(v.scalar_tensor_tensor(
                            xb[0][:, tp, h * 128:(h + 1) * 128],
                            yb[:, tp % 2, h, 0:128],
                            rcp16[:, h:h + 1],
                            xb[0][:, tp, h * 128:(h + 1) * 128],
                            ALU.mult, ALU.add))
                    emit_ln(v, st, xb[0][:, tp, :],
                            lambda tp=tp: v.wait_ge(sa.h, sa_b + 2 * tp + 2))

            b0_stt_end = [bp0(tp, H + 1) for tp in range(4)]

            @block.scalar
            def _(s):
                def st(ins):
                    sa.inc(ins)
                    s.wait_ge(sa.h, sa.n)
                barrier_waits(s, "scalar")
                for tp in range(4):
                    s.wait_ge(sv.h, bp0(tp, H + LN_OPS))
                    st(s.activation(vc[:], vc[:], AF.Ln, bias=eps_ln[:]))
                    st(s.activation(rsc[:], vc[:], AF.Exp, scale=-0.5))

            pe_bT = sp.n
            for g in range(4 * NDT):
                trp_pe.append(pe_bT + g + 1)
                trp_ev.append(("sa", sa.n + g + 1))

            @block.tensor
            def _(t):
                barrier_waits(t, "tensor")
                g = 0
                for tq in range(4):
                    for dt in range(NDT):
                        if dt == 0:
                            t.wait_ge(sv.h, bp0(tq, H + 1 + LN_OPS))
                        if g >= 2:
                            k_, n_ = trp_ev[g - 2]
                            t.wait_ge(sa.h if k_ == "sa" else sv.h, n_)
                        sp.inc(t.transpose(ps_tf[g % 2][:],
                                           xb[0][:, tq, dt * 128:(dt + 1) * 128],
                                           id_f[:]))
                        assert sp.n == trp_pe[g]
                        g += 1

            @block.scalar
            def _(s):
                g = 0
                for tq in range(4):
                    for dt in range(NDT):
                        s.wait_ge(sp.h, trp_pe[g])
                        sa.inc(s.copy(X1T[0][:, dt, tq * 128:(tq + 1) * 128],
                                      ps_tf[g % 2][:]))
                        assert ("sa", sa.n) == trp_ev[g]
                        g += 1

            gtb = 4 * NDT
            x1t_ready = sa.n
            prev = {"b_stt_end3": b0_stt_end[3]}
            snap_marks()

            for c in range(NQC):
                pe_c, sa_c, sv_c, dxy_c = sp.n, sa.n, sv.n, dXY2.n
                dw1_c = [c_.n for c_ in dW1]
                dw2_c = [c_.n for c_ in dW2]
                has_b = c + 1 < NQC
                nxt = (c + 1) % 2

                ch_ins = {(2 * tp + 1) * NHC // 8: tp for tp in range(4)} if has_b else {}
                bt_ins = {}
                if has_b:
                    for i_ in range(4):
                        bt_ins.setdefault(max(0, (i_ + 1) * NDT // 4 - 1), []).append(i_)

                p = 0
                m1_end = {}
                den_pos = {}
                m2_pos = {}
                m2_end = {}
                ctr_pe = {}
                gt0_c = gtb
                gtw = gtb
                bt_pe = {}
                for hc in range(NHC):
                    p += 1; m1_end[hc] = p
                    if hc >= 1:
                        p += 1; den_pos[hc - 1] = p
                p += 1; den_pos[NHC - 1] = p
                p += 1; bcast_pos = p
                m2_q = {}
                m2_jobs = {}
                for dt in range(NDT):
                    jl = []
                    if dt >= 2:
                        jl += [("c", dt - 2, tq) for tq in range(4)]
                    for i_ in bt_ins.get(dt, []):
                        jl += [("b", NDT * i_ + j_, 0) for j_ in range(NDT)]
                    if dt == NDT - 1:
                        jl += [("c", NDT - 2, tq) for tq in range(4)]
                    m2_jobs[dt] = jl
                m2_seq = {}
                for dt in range(NDT):
                    jl = m2_jobs[dt]
                    span = max(NHC - 4, 1)
                    ins_after = [0] * NHC
                    for x in range(len(jl)):
                        ins_after[min((x * span) // max(len(jl), 1), NHC - 5)] += 1
                    seq = []
                    ji = 0
                    for hc in range(NHC):
                        seq.append(("mm", hc))
                        for _ in range(ins_after[hc]):
                            seq.append(("job", jl[ji])); ji += 1
                    assert ji == len(jl)
                    if dt == NDT - 1:
                        seq += [("job", ("c", NDT - 1, tq)) for tq in range(4)]
                    m2_seq[dt] = seq
                    m2_pos[dt] = p
                    qn = 0
                    for op in seq:
                        if op[0] == "mm":
                            if (op[1] + 1) % (NHC // 4) == 0:
                                p += 1
                                m2_q[(dt, qn)] = p
                                qn += 1
                                if qn == 4:
                                    m2_end[dt] = p
                        else:
                            kindc, a1, a2 = op[1]
                            p += 1
                            trp_pe.append(pe_c + p)
                            if kindc == "c":
                                ctr_pe[(a1, a2)] = gtw
                            else:
                                bt_pe[a1] = gtw
                            gtw += 1
                pe_total_c = p

                d = 0
                d_tt = {}
                d_hsq = {}
                b_var = {}
                b_apply = {}
                b_stt_end = {}
                for hc in range(NHC):
                    d += 1; d_tt[hc] = sv_c + d
                    d += 1; d_hsq[hc] = sv_c + d
                    if hc in ch_ins:
                        tp = ch_ins[hc]
                        b_stt_end[tp] = sv_c + d + 1 + H
                        b_var[tp] = sv_c + d + H + LN_OPS
                        b_apply[tp] = sv_c + d + CHUNK_OPS
                        d += CHUNK_OPS
                d += 1; d_bcev = sv_c + d
                d_ev = {}
                d_add = {}
                for dt in range(NDT):
                    d += 1; d_ev[dt] = sv_c + d
                    for tq in range(4):
                        d += 1; d_add[(dt, tq)] = sv_c + d
                d_ln2 = {}
                for tq in range(4):
                    for j in range(1, LN_OPS + 1):
                        d += 1; d_ln2[(tq, j)] = sv_c + d

                a = 0
                a_relu = {}
                a_bln = {}
                for hc in range(NHC):
                    a += 1; a_relu[hc] = sa_c + a
                    if hc in ch_ins:
                        a += 2; a_bln[ch_ins[hc]] = sa_c + a
                a += 2; a_rms = sa_c + a
                a_xev = {}
                if has_b:
                    for g in range(4 * NDT):
                        a += 1; a_xev[g] = sa_c + a
                a_cln = {}
                for tq in range(4):
                    a += 2; a_cln[tq] = sa_c + a
                tmp_ev = {}
                for (dtc, tq), gti in ctr_pe.items():
                    tmp_ev[gti] = ("sv", d_add[(dtc, tq)])
                for jobi, gti in bt_pe.items():
                    tmp_ev[gti] = ("sa", a_xev[jobi])
                while len(trp_ev) < gtw:
                    trp_ev.append(None)
                for gti in range(gt0_c, gtw):
                    trp_ev[gti] = tmp_ev[gti]
                gtb = gtw

                @block.gpsimd
                def _(gp, c=c, has_b=has_b, nxt=nxt, dxy_c=dxy_c, prev=prev,
                      d_ln2=d_ln2, b_stt_end=b_stt_end):
                    if has_b:
                        if c >= 1:
                            gp.wait_ge(dOUT.h, 16 * c)
                        gp.wait_ge(sv.h, prev["b_stt_end3"])
                        dXY2.inc(gp.dma_start(
                            out=xb[nxt][:, :, :],
                            in_=x_hbm[(c + 1) * 512:(c + 2) * 512, :]
                            .rearrange("(a p) d -> p a d", p=128)))
                        for tp in range(2):
                            for h in range(H):
                                dXY2.inc(gp.dma_start(
                                    out=yb[:, tp % 2, h, :],
                                    in_=ys[h].rearrange("(a p) e -> p a e", p=128)
                                    [:, 4 * (c + 1) + tp, :]))
                        gp.wait_ge(sv.h, b_stt_end[1])
                        for tp in range(2, 4):
                            for h in range(H):
                                dXY2.inc(gp.dma_start(
                                    out=yb[:, tp % 2, h, :],
                                    in_=ys[h].rearrange("(a p) e -> p a e", p=128)
                                    [:, 4 * (c + 1) + tp, :]))
                    gp.wait_ge(sv.h, d_ln2[(3, LN_OPS)])
                    dOUT.inc(gp.dma_start(
                        out=out_hbm[c * 512:(c + 1) * 512, :]
                        .rearrange("(a p) d -> p a d", p=128),
                        in_=xb[c % 2][:, :, :]))

                @block.sync
                def _(sy, c=c, prev=prev):
                    if c == 0:
                        barrier_waits(sy, "sync")
                        sy.wait_ge(dSCR[0].h, 16 * ((NTT + 1) // 2))
                        sy.wait_ge(dSCR[1].h, 16 * (NTT // 2))
                    for dt in range(NDT):
                        for q_ in range(4):
                            if q_ >= 2:
                                sy.wait_ge(sp.h, pe_c + m2_q[(dt, q_ - 2)])
                            elif dt >= 1:
                                sy.wait_ge(sp.h, pe_c + m2_q[(dt - 1,
                                                              min(q_ + 2, 3))])
                            elif c >= 1:
                                sy.wait_ge(sp.h, prev["m2_q15"][q_])
                            dW2[q_].inc(sy.dma_start(
                                out=w2b[q_ % 2][:, :],
                                in_=w2t[dt, :, q_ * (HID // 4):(q_ + 1) * (HID // 4)]))

                @block.tensor
                def _(t, c=c, has_b=has_b, nxt=nxt, prev=prev, dw1_c=dw1_c,
                      dw2_c=dw2_c, m1_end=m1_end, den_pos=den_pos,
                      bcast_pos=bcast_pos, m2_pos=m2_pos, m2_end=m2_end,
                      d_hsq=d_hsq, d_ev=d_ev, b_apply=b_apply, a_relu=a_relu,
                      a_rms=a_rms, ch_ins=ch_ins, bt_ins=bt_ins,
                      ctr_pe=ctr_pe, bt_pe=bt_pe, gt0_c=gt0_c,
                      m2_seq=m2_seq):
                    if c == 0:
                        barrier_waits(t, "tensor")
                        t.wait_ge(sa.h, x1t_ready)
                    else:
                        t.wait_ge(sa.h, prev["x1t_ready"])
                    seen_bq = set()
                    for hc in range(NHC):
                        jg = c * NHC + hc
                        t.wait_ge(dW1[jg % W1K].h, 16 * (jg // W1K + 1))
                        if hc >= 2:
                            t.wait_ge(sa.h, a_relu[hc - 2])
                        elif c >= 1:
                            t.wait_ge(sa.h, prev["a_relu"][NHC - 2 + hc])
                        for dt in range(NDT):
                            ins = t.matmul(ps_h[hc % 2][:, :],
                                           w1b[(c * NHC + hc) % W1K][:, dt * 128:(dt + 1) * 128],
                                           X1T[0][:, dt, :],
                                           start=(dt == 0), stop=(dt == NDT - 1))
                            if dt == NDT - 1:
                                sp.inc(ins)
                        if hc >= 1:
                            dn = hc - 1
                            t.wait_ge(sv.h, d_hsq[dn])
                            if dn == 0 and c >= 1:
                                t.wait_ge(sa.h, prev["a_rms"])
                            sp.inc(t.matmul(ps_den[0:1, :], ones_bf[:],
                                            hsq[dn % 2][:, :],
                                            start=(dn == 0), stop=(dn == NHC - 1)))
                        assert sp.n == pe_c + m1_end[hc] + (1 if hc >= 1 else 0)
                    t.wait_ge(sv.h, d_hsq[NHC - 1])
                    sp.inc(t.matmul(ps_den[0:1, :], ones_bf[:],
                                    hsq[(NHC - 1) % 2][:, :],
                                    start=False, stop=True))
                    t.wait_ge(sa.h, a_rms)
                    if c >= 1:
                        t.wait_ge(sv.h, prev["d_bcev"])
                    for tq in range(4):
                        ins = t.matmul(ps_bc[:, tq:tq + 1],
                                       lnt[0:1, tq * 128:(tq + 1) * 128],
                                       ones_f[0:1, 0:1],
                                       start=(tq == 0), stop=(tq == 3))
                        if tq == 3:
                            sp.inc(ins)
                    assert sp.n == pe_c + bcast_pos
                    seen_ct = set()
                    for dt in range(NDT):
                        for op in m2_seq[dt]:
                            if op[0] == "mm":
                                hc = op[1]
                                q_ = hc // (NHC // 4)
                                off = hc - q_ * (NHC // 4)
                                if off == 0:
                                    t.wait_ge(dW2[q_].h, dw2_c[q_] + 16 * (dt + 1))
                                if hc == 0:
                                    if dt >= 2:
                                        t.wait_ge(sv.h, d_ev[dt - 2])
                                    elif c >= 1:
                                        t.wait_ge(sv.h, prev["d_ev"][NDT - 2 + dt])
                                ins = t.matmul(
                                    ps_m[dt % 2][:, :],
                                    w2b[q_ % 2][:, off * 128:(off + 1) * 128],
                                    hbuf[:, hc, :],
                                    start=(hc == 0), stop=(hc == NHC - 1))
                                if (hc + 1) % (NHC // 4) == 0:
                                    sp.inc(ins)
                            else:
                                kindc, a1, a2 = op[1]
                                if kindc == "c":
                                    gti = ctr_pe[(a1, a2)]
                                    if a1 not in seen_ct:
                                        t.wait_ge(sv.h, d_ev[a1])
                                        seen_ct.add(a1)
                                    src = yTd[a1 % 2][:, a2 * 128:(a2 + 1) * 128]
                                else:
                                    gti = bt_pe[a1]
                                    tq, dtp = a1 // NDT, a1 % NDT
                                    if tq not in seen_bq:
                                        t.wait_ge(sv.h, b_apply[tq])
                                        seen_bq.add(tq)
                                    src = xb[nxt][:, tq, dtp * 128:(dtp + 1) * 128]
                                if gti >= 2:
                                    k_, n_ = trp_ev[gti - 2]
                                    t.wait_ge(sa.h if k_ == "sa" else sv.h, n_)
                                sp.inc(t.transpose(ps_tf[gti % 2][:], src, id_f[:]))
                                assert sp.n == trp_pe[gti]

                @block.scalar
                def _(s, c=c, has_b=has_b, nxt=nxt, m1_end=m1_end,
                      den_pos=den_pos, d_tt=d_tt, b_var=b_var, ch_ins=ch_ins,
                      d_ln2=d_ln2, bt_pe=bt_pe, a_xev=a_xev):
                    def st(ins):
                        sa.inc(ins)
                        s.wait_ge(sa.h, sa.n)

                    def w1_load(j):
                        dW1[j % W1K].inc(s.dma_start(out=w1b[j % W1K][:, :],
                                                     in_=w1t[j % NHC, :, :]))
                    if c == 0:
                        barrier_waits(s, "scalar")
                        for j in range(W1K):
                            w1_load(j)
                    for hc in range(NHC):
                        s.wait_ge(sp.h, pe_c + m1_end[hc])
                        if hc >= 2:
                            s.wait_ge(sv.h, d_hsq[hc - 2])
                        sa.inc(s.activation(trel[hc % 2][:, :], ps_h[hc % 2][:, :],
                                            AF.Relu))
                        j = c * NHC + hc + W1K
                        if j < (c + 1) * NHC or has_b:
                            w1_load(j)
                        if hc in ch_ins:
                            tp = ch_ins[hc]
                            s.wait_ge(sv.h, b_var[tp])
                            st(s.activation(vc[:], vc[:], AF.Ln, bias=eps_ln[:]))
                            st(s.activation(rsc[:], vc[:], AF.Exp, scale=-0.5))
                    s.wait_ge(sp.h, pe_c + den_pos[NHC - 1])
                    st(s.activation(lnt[:, :], ps_den[0:1, :], AF.Ln,
                                    bias=eps_rms[0:1, :], scale=1.0 / HID))
                    st(s.activation(lnt[:, :], lnt[:, :], AF.Exp, scale=-0.5))
                    if has_b:
                        for jobi in range(4 * NDT):
                            tq, dtp = jobi // NDT, jobi % NDT
                            gti = bt_pe[jobi]
                            s.wait_ge(sp.h, trp_pe[gti])
                            sa.inc(s.copy(X1T[0][:, dtp, tq * 128:(tq + 1) * 128],
                                          ps_tf[gti % 2][:]))
                            assert ("sa", sa.n) == trp_ev[gti]
                    for tq in range(4):
                        s.wait_ge(sv.h, d_ln2[(tq, LN_OPS - 1)])
                        st(s.activation(vc[:], vc[:], AF.Ln, bias=eps_ln[:]))
                        st(s.activation(rsc[:], vc[:], AF.Exp, scale=-0.5))

                @block.vector
                def _(v, c=c, has_b=has_b, nxt=nxt, dxy_c=dxy_c, a_relu=a_relu,
                      den_pos=den_pos, bcast_pos=bcast_pos, m2_end=m2_end,
                      ctr_pe=ctr_pe, ch_ins=ch_ins, a_bln=a_bln, a_cln=a_cln,
                      prev=prev, d_ev=d_ev, d_add=d_add):
                    def st(ins):
                        sv.inc(ins)
                        v.wait_ge(sv.h, sv.n)
                    if c == 0:
                        barrier_waits(v, "vector")
                    for hc in range(NHC):
                        v.wait_ge(sa.h, a_relu[hc])
                        if hc >= 2:
                            v.wait_ge(sp.h, pe_c + den_pos[hc - 2])
                        elif c >= 1:
                            v.wait_ge(sp.h, prev["pe_c"] + prev["den_pos"][NHC - 2 + hc])
                        st(v.tensor_tensor(hbuf[:, hc, :], trel[hc % 2][:, :],
                                           trel[hc % 2][:, :], ALU.mult))
                        st(v.tensor_tensor(hsq[hc % 2][:, :], hbuf[:, hc, :],
                                           hbuf[:, hc, :], ALU.mult))
                        if hc in ch_ins:
                            tp = ch_ins[hc]
                            v.wait_ge(dXY2.h, dxy_c + 16 * (1 + (2 if tp < 2 else 4) * H))
                            st(v.reciprocal(rcp16[:, :], yb[:, tp % 2, :, 128]))
                            for h in range(H):
                                st(v.scalar_tensor_tensor(
                                    xb[nxt][:, tp, h * 128:(h + 1) * 128],
                                    yb[:, tp % 2, h, 0:128],
                                    rcp16[:, h:h + 1],
                                    xb[nxt][:, tp, h * 128:(h + 1) * 128],
                                    ALU.mult, ALU.add))
                            emit_ln(v, st, xb[nxt][:, tp, :],
                                    lambda tp=tp: v.wait_ge(sa.h, a_bln[tp]))
                            assert sv.n == b_apply[tp]
                    v.wait_ge(sp.h, pe_c + bcast_pos)
                    st(v.tensor_copy(rsb4[:, :], ps_bc[:, 0:4]))
                    assert sv.n == d_bcev
                    for dt in range(NDT):
                        v.wait_ge(sp.h, pe_c + m2_end[dt])
                        if dt >= 2:
                            v.wait_ge(sp.h, trp_pe[ctr_pe[(dt - 2, 3)]])
                        elif c >= 1:
                            v.wait_ge(sp.h, prev["ctr_last_pe"][dt])
                        st(v.tensor_copy(yTd[dt % 2][:, :], ps_m[dt % 2][:, :]))
                        assert sv.n == d_ev[dt]
                        for tq in range(4):
                            v.wait_ge(sp.h, trp_pe[ctr_pe[(dt, tq)]])
                            st(v.scalar_tensor_tensor(
                                xb[c % 2][:, tq, dt * 128:(dt + 1) * 128],
                                ps_tf[ctr_pe[(dt, tq)] % 2][:],
                                rsb4[:, tq:tq + 1],
                                xb[c % 2][:, tq, dt * 128:(dt + 1) * 128],
                                ALU.mult, ALU.add))
                    for tq in range(4):
                        emit_ln(v, st, xb[c % 2][:, tq, :],
                                lambda tq=tq: v.wait_ge(sa.h, a_cln[tq]))
                        assert sv.n == d_ln2[(tq, LN_OPS)]

                prev = {
                    "b_stt_end3": b_stt_end.get(3, prev.get("b_stt_end3", 0)),
                    "x1t_ready": (a_xev[4 * NDT - 1] if has_b else 0),
                    "a_relu": a_relu,
                    "a_rms": a_rms,
                    "d_bcev": d_bcev,
                    "d_ev": d_ev,
                    "pe_c": pe_c,
                    "den_pos": den_pos,
                    "ctr_last_pe": {dtx: trp_pe[ctr_pe[(NDT - 2 + dtx, 3)]]
                                    for dtx in range(2)},
                    "m2_q15": [pe_c + m2_q[(NDT - 1, min(q_ + 2, 3))]
                               for q_ in range(2)],
                }

        @block.gpsimd
        def _(gp):
            gp.wait_ge(dOUT.h, 16 * NQC)

    return nc


def kernel(L0_attn_x, L0_ffn_W_fc, L0_ffn_W_proj):
    x = np.ascontiguousarray(np.asarray(L0_attn_x, dtype=np.float32))
    wfc = np.ascontiguousarray(np.asarray(L0_ffn_W_fc, dtype=np.float32))
    wproj = np.ascontiguousarray(np.asarray(L0_ffn_W_proj, dtype=np.float32))
    B = x.shape[0]
    nc = build_program(L=x.shape[1], D=x.shape[2], HID=wfc.shape[0])
    in_maps = [{"x": x[b], "wfc": wfc, "wproj": wproj} for b in range(B)]
    res = run_bass_kernel_spmd(nc, in_maps, core_ids=list(range(B)))
    return np.stack([res.results[b]["out"] for b in range(B)], axis=0)


# version 22
# speedup vs baseline: 1.1535x; 1.1535x over previous
import numpy as np
from contextlib import ExitStack

import concourse.bass as bass
import concourse.mybir as mybir
from concourse.bass_utils import run_bass_kernel_spmd

F32 = mybir.dt.float32
BF16 = mybir.dt.bfloat16
AF = mybir.ActivationFunctionType
ALU = mybir.AluOpType
AX = mybir.AxisListType

EXP_SCALE = float(np.float32(1.0) / np.sqrt(np.float32(128.0)))
EXP_SHIFT = -10.0
EPS_LN = 1e-5
EPS_RMS = float(np.finfo(np.float32).eps)
G = 64


class Ctr:
    def __init__(self, sem, step=1):
        self.h = sem
        self.n = 0
        self.step = step

    def inc(self, ins):
        ins.then_inc(self.h, self.step)
        self.n += self.step
        return self.n


def build_program(L=4096, D=2048, HID=8192):
    H = D // 128
    NT = L // 128
    NQC = L // 512
    NDT = D // 128
    NHC = HID // 128
    W2F = min(2048, HID)
    NW2C = HID // W2F
    NT1 = HID // 128
    NTT = NT1 + NDT * NW2C
    assert L % 512 == 0 and D % 128 == 0 and HID % 128 == 0 and NT % 4 == 0

    nc = bass.Bass("TRN2", target_bir_lowering=False)
    x_hbm = nc.dram_tensor("x", [L, D], F32, kind="ExternalInput").ap()
    wfc = nc.dram_tensor("wfc", [HID, D], F32, kind="ExternalInput").ap()
    wproj = nc.dram_tensor("wproj", [D, HID], F32, kind="ExternalInput").ap()
    out_hbm = nc.dram_tensor("out", [L, D], F32, kind="ExternalOutput").ap()
    w1t = nc.dram_tensor("w1t", [NHC, 128, D], BF16, kind="Internal").ap()
    w2t = nc.dram_tensor("w2t", [NDT, 128, HID], BF16, kind="Internal").ap()
    ys = nc.dram_tensor("ys", [H, L, 129], F32, kind="Internal").ap()

    ctx = ExitStack()
    with ctx:
        ec = ctx.enter_context

        id_bf = ec(nc.sbuf_tensor("id_bf", [128, 128], BF16))
        id_f = ec(nc.sbuf_tensor("id_f", [128, 128], F32))
        tri = ec(nc.sbuf_tensor("tri", [128, 128], BF16))
        mneg = ec(nc.sbuf_tensor("mneg", [128, 128], BF16))
        nbias = ec(nc.sbuf_tensor("nbias", [128, 1], F32))
        eps_ln = ec(nc.sbuf_tensor("eps_ln", [128, 1], F32))
        eps_rms = ec(nc.sbuf_tensor("eps_rms", [128, 1], F32))
        ones_bf = ec(nc.sbuf_tensor("ones_bf", [128, 1], BF16))
        ones_f = ec(nc.sbuf_tensor("ones_f", [1, 32], F32))

        sp = Ctr(ec(nc.semaphore("sp")))
        sa = Ctr(ec(nc.semaphore("sa")))
        sv = Ctr(ec(nc.semaphore("sv")))
        sg = Ctr(ec(nc.semaphore("sg")))
        dW = [Ctr(ec(nc.semaphore(f"dW{i}")), 16) for i in range(2)]
        dSCR = [Ctr(ec(nc.semaphore(f"dSCR{i}")), 16) for i in range(2)]
        dX = [Ctr(ec(nc.semaphore(f"dX{i}")), 16) for i in range(2)]
        dYS = [Ctr(ec(nc.semaphore(f"dYS{i}")), 16) for i in range(2)]
        dXY = Ctr(ec(nc.semaphore("dXY")), 16)
        dXY2 = Ctr(ec(nc.semaphore("dXY2")), 16)
        dW1 = [Ctr(ec(nc.semaphore(f"dW1_{i}")), 16) for i in range(8)]
        dW2 = [Ctr(ec(nc.semaphore(f"dW2_{i}")), 16) for i in range(4)]
        dOUT = Ctr(ec(nc.semaphore("dOUT")), 16)

        block = ec(nc.Block())

        chain = {"tensor": sp, "vector": sv, "scalar": sa, "gpsimd": sg}
        marks = {e: 0 for e in chain}

        def snap_marks():
            for e in chain:
                marks[e] = chain[e].n

        def barrier_waits(eng, me):
            for e, c in chain.items():
                if e != me and marks[e] > 0:
                    eng.wait_ge(c.h, marks[e])

        @block.gpsimd
        def _(gp):
            def st(ins):
                sg.inc(ins)
                gp.wait_ge(sg.h, sg.n)
            st(gp.memset(id_bf[:], 0.0))
            st(gp.affine_select(out=id_bf[:], in_=id_bf[:], compare_op=ALU.not_equal,
                                fill=1.0, base=0, pattern=[[-1, 128]], channel_multiplier=1))
            st(gp.memset(id_f[:], 0.0))
            st(gp.affine_select(out=id_f[:], in_=id_f[:], compare_op=ALU.not_equal,
                                fill=1.0, base=0, pattern=[[-1, 128]], channel_multiplier=1))
            st(gp.memset(tri[:], 0.0))
            st(gp.affine_select(out=tri[:], in_=tri[:], compare_op=ALU.is_gt,
                                fill=1.0, base=0, pattern=[[-1, 128]], channel_multiplier=1))
            st(gp.memset(mneg[:], -3000.0))
            st(gp.affine_select(out=mneg[:], in_=mneg[:], compare_op=ALU.is_gt,
                                fill=0.0, base=0, pattern=[[-1, 128]], channel_multiplier=1))
            sg.inc(gp.memset(nbias[:], EXP_SHIFT))
            sg.inc(gp.memset(eps_ln[:], EPS_LN))
            sg.inc(gp.memset(eps_rms[:], EPS_RMS))
            sg.inc(gp.memset(ones_bf[:], 1.0))
            sg.inc(gp.memset(ones_f[:], 1.0))

        INIT_N0 = sg.n

        t_tiles = []
        for i in range(NT1):
            t_tiles.append((wfc[i * 128:(i + 1) * 128, :], w1t[i, :, :], D))
        for dt in range(NDT):
            for c in range(NW2C):
                t_tiles.append((wproj[dt * 128:(dt + 1) * 128, c * W2F:(c + 1) * W2F],
                                w2t[dt, :, c * W2F:(c + 1) * W2F], W2F))
        assert len(t_tiles) == NTT

        s1 = ExitStack()
        with s1:
            e1 = s1.enter_context
            wtf = [e1(nc.sbuf_tensor(f"wtf{i}", [128, 2048], F32)) for i in range(2)]
            wbb = [e1(nc.sbuf_tensor(f"wbb{i}", [128, 2048], BF16)) for i in range(2)]
            tgs = [e1(nc.sbuf_tensor(f"tgs{i}", [128, 2048 // G], F32)) for i in range(2)]
            tsc = [e1(nc.sbuf_tensor(f"tsc{i}", [128, 2048 // G], BF16)) for i in range(2)]
            trc = [e1(nc.sbuf_tensor(f"trc{i}", [128, 2048 // G], F32)) for i in range(2)]
            trr = [e1(nc.sbuf_tensor(f"trr{i}", [128, 2048], BF16)) for i in range(2)]
            tg1 = [e1(nc.sbuf_tensor(f"tg1{i}", [128, 2048], BF16)) for i in range(2)]
            tg2 = [e1(nc.sbuf_tensor(f"tg2{i}", [128, 2048], BF16)) for i in range(2)]
            tqt = [e1(nc.sbuf_tensor(f"tqt{i}", [128, 2048], BF16)) for i in range(4)]
            tst = [e1(nc.sbuf_tensor(f"tst{i}", [128, 16, 128], BF16,
                                       is_transpose_dest=True)) for i in range(2)]
            xres = [e1(nc.sbuf_tensor(f"xres{i}", [128, NT, 129], BF16)) for i in range(2)]
            xT = [e1(nc.sbuf_tensor(f"xT{i}", [128, L], BF16)) for i in range(2)]
            expS = [e1(nc.sbuf_tensor(f"expS{i}", [128, 1024], BF16)) for i in range(2)]
            ystg = [e1(nc.sbuf_tensor(f"ystg{i}", [128, 4, 129], F32)) for i in range(2)]
            ps_tr = [e1(nc.psum_tensor(f"ps_tr{i}", [128, 512], BF16)) for i in range(2)]
            ps_s = [e1(nc.psum_tensor(f"ps_s{i}", [128, 1024], F32)) for i in range(2)]
            ps_y = [e1(nc.psum_tensor(f"ps_y{i}", [128, 258], F32)) for i in range(2)]

            @block.gpsimd
            def _(gp):
                sg.inc(gp.memset(xres[0][:, :, 128:129], 1.0))
                sg.inc(gp.memset(xres[1][:, :, 128:129], 1.0))

            INIT_N = sg.n

            assert NTT % 2 == 0
            n_pairs = NTT // 2
            head_pairs = [[] for _ in range(H)]
            for p_ in range(n_pairs):
                head_pairs[(p_ * H) // n_pairs].append(p_)

            t_copy_pos = {}
            t_done_pos = {}
            t_ev_end = {}
            t_pe_end = {}
            tr_evict_pos = []
            tr_pe_pos = []

            def t_dve_ops(i, F):
                b = i % 2
                b4 = i % 4
                ng = F // G
                return [
                    lambda v: v.tensor_copy(wbb[b][:, 0:F], wtf[b][:, 0:F]),
                    lambda v: v.tensor_reduce(
                        tgs[b].rearrange("p (g o) -> p g o", o=1)[:, 0:ng, :],
                        wbb[b].rearrange("p (g e) -> p g e", e=G)[:, 0:ng, :],
                        AX.X, ALU.add, apply_absolute_value=True),
                    lambda v: v.tensor_scalar(tsc[b][:, 0:ng], tgs[b][:, 0:ng],
                                              1.0 / G, 1e-8, ALU.mult, ALU.max),
                    lambda v: v.reciprocal(trc[b][:, 0:ng], tsc[b][:, 0:ng]),
                    lambda v: v.tensor_tensor(
                        trr[b].rearrange("p (g e) -> p g e", e=G)[:, 0:ng, :],
                        wbb[b].rearrange("p (g e) -> p g e", e=G)[:, 0:ng, :],
                        trc[b].rearrange("p (g o) -> p g o", o=1)[:, 0:ng, :]
                           .broadcast_to([128, ng, G]),
                        ALU.mult),
                    lambda v: v.tensor_scalar(tg1[b][:, 0:F], trr[b][:, 0:F],
                                              0.5, None, ALU.is_gt),
                    lambda v: v.tensor_scalar(tg2[b][:, 0:F], trr[b][:, 0:F],
                                              -0.5, None, ALU.is_lt),
                    lambda v: v.tensor_sub(tqt[b4][:, 0:F], tg1[b][:, 0:F],
                                           tg2[b][:, 0:F]),
                    lambda v: v.tensor_tensor(
                        tqt[b4].rearrange("p (g e) -> p g e", e=G)[:, 0:ng, :],
                        tqt[b4].rearrange("p (g e) -> p g e", e=G)[:, 0:ng, :],
                        tsc[b].rearrange("p (g o) -> p g o", o=1)[:, 0:ng, :]
                           .broadcast_to([128, ng, G]),
                        ALU.mult),
                ]

            def emit_t_chunk(v, p_):
                ia, ib = 2 * p_, 2 * p_ + 1
                opsa = t_dve_ops(ia, t_tiles[ia][2])
                opsb = t_dve_ops(ib, t_tiles[ib][2])
                v.wait_ge(dW[0].h, 16 * (p_ + 1))
                v.wait_ge(dW[1].h, 16 * (p_ + 1))
                for k in range(9):
                    for i, ops in ((ia, opsa), (ib, opsb)):
                        if k > 0:
                            v.wait_ge(sv.h, sv.n - 1)
                        if k == 7 and i >= 4:
                            v.wait_ge(sp.h, t_pe_end[i - 4])
                        sv.inc(ops[k](v))
                        if k == 0:
                            assert sv.n == t_copy_pos[i]
                        if k == 8:
                            assert sv.n == t_done_pos[i]

            def emit_t_sync_loads(sy, p_):
                for i in (2 * p_, 2 * p_ + 1):
                    if i >= 2:
                        sy.wait_ge(sv.h, t_copy_pos[i - 2])
                    dW[i % 2].inc(sy.dma_start(out=wtf[i % 2][:, 0:t_tiles[i][2]],
                                               in_=t_tiles[i][0]))

            def emit_t_scratch(q, p_):
                for i in (2 * p_, 2 * p_ + 1):
                    F = t_tiles[i][2]
                    q.wait_ge(sa.h, t_ev_end[i])
                    dSCR[i % 2].inc(q.dma_start(out=t_tiles[i][1],
                                                in_=tst[i % 2][:, 0:F // 128, :]))

            def head_x_src(h):
                full = x_hbm[:, h * 128:(h + 1) * 128].rearrange("(a p) d -> p a d", p=128)
                return [full[:, q * (NT // 4):(q + 1) * (NT // 4), :] for q in range(4)]

            @block.gpsimd
            def _(gp):
                barrier_waits(gp, "gpsimd")
                for h in range(min(2, H)):
                    for q, src in enumerate(head_x_src(h)):
                        dX[h % 2].inc(gp.dma_start(
                            out=xres[h % 2][:, q * (NT // 4):(q + 1) * (NT // 4), 0:128],
                            in_=src))

            exp_abs = {}
            pe_head_end = []
            gks = 0

            xt_ready = {}
            prev_yev_end = 0
            exp_abs_g = {}
            ggs = 0

            for h in range(H):
                pe_mark_h = sp.n
                act_mark_h = sa.n
                dve_mark_h = sv.n
                gtr0_h = len(tr_pe_pos)

                hp = head_pairs[h]
                qs = {}
                for idx, p_ in enumerate(hp):
                    if len(hp) <= max(NQC - 2, 1):
                        qc_ = min(2, NQC - len(hp)) + idx
                    else:
                        qc_ = idx // 2
                    qs[p_] = min(qc_, NQC - 1)
                slot_chunks = {qc: [] for qc in range(NQC)}
                for p_ in hp:
                    slot_chunks[qs[p_]].append(p_)
                pairs_by_region = {qc: [] for qc in range(NQC + 1)}
                for p_ in hp:
                    pairs_by_region[min(qs[p_] + 1, NQC)].append(p_)
                region_jobs = {qc: [] for qc in range(NQC + 1)}
                for p_ in hp:
                    reg = min(qs[p_] + 2, NQC)
                    for i in (2 * p_, 2 * p_ + 1):
                        nj = t_tiles[i][2] // 128
                        for j in range(0, nj, 4):
                            region_jobs[reg].append(("t", i, j, min(4, nj - j)))
                jobs_x = ([("x", h + 1, j, min(4, NT - j))
                           for j in range(0, NT, 4)] if h + 1 < H else [])
                wts = [5 * (4 * qc + 4) - 6 for qc in range(NQC)]
                tot = sum(wts)
                acc = 0
                taken = 0
                for qc in range(NQC):
                    acc += wts[qc]
                    upto = (len(jobs_x) * acc) // tot
                    region_jobs[qc] = jobs_x[taken:upto] + region_jobs[qc]
                    taken = upto
                prologue_jobs = ([("x", 0, j, min(4, NT - j))
                                  for j in range(0, NT, 4)] if h == 0 else [])

                qc_groups = []
                for qc in range(NQC):
                    gl = []
                    if qc >= 1:
                        gl += [[2 * i, 2 * i + 1] for i in range(2 * qc)]
                    gl += [[4 * qc + t0] for t0 in range(4)]
                    qc_groups.append(gl)

                pos = 0
                for j_ in prologue_jobs:
                    pos += 1
                    tr_pe_pos.append(pe_mark_h + pos)
                qc_plan = []
                for qc in range(NQC):
                    nk = 4 * qc + 4
                    groups = qc_groups[qc]
                    ng = len(groups)
                    pv_lists = [list(range(max(0, kt - 4 * qc), 4)) for kt in range(nk)]
                    jobs = region_jobs[qc]
                    share = [0] * ng
                    for x in range(len(jobs)):
                        share[(x * ng) // len(jobs)] += 1
                    seq = [("s", 0), ("s", 1)]
                    ji = 0
                    for gi in range(ng):
                        seq.append(("pv", gi))
                        for _ in range(share[gi]):
                            seq.append(("tr", jobs[ji])); ji += 1
                        if gi + 2 < ng:
                            seq.append(("s", gi + 2))
                    assert ji == len(jobs)
                    s_end = [0] * ng
                    pv_end_g = [0] * ng
                    for op in seq:
                        if op[0] == "s":
                            pos += 1
                            s_end[op[1]] = pos
                        elif op[0] == "pv":
                            group = groups[op[1]]
                            if len(group) == 1 and (group[0] - 4 * qc) % 2 == 1:
                                pos += 1
                            pv_end_g[op[1]] = pos
                        else:
                            pos += 1
                            tr_pe_pos.append(pe_mark_h + pos)
                            kind, a1, a2, n_ = op[1]
                            if kind == "t" and a2 + n_ == t_tiles[a1][2] // 128:
                                t_pe_end[a1] = pe_mark_h + pos
                    qc_plan.append((groups, seq, s_end, pv_end_g, pv_lists))
                    pos = pos
                for op1 in region_jobs[NQC]:
                    pos += 1
                    tr_pe_pos.append(pe_mark_h + pos)
                    kind, a1, a2, n_ = op1
                    if kind == "t" and a2 + n_ == t_tiles[a1][2] // 128:
                        t_pe_end[a1] = pe_mark_h + pos

                a = 0
                for j_ in prologue_jobs:
                    a += 1
                    tr_evict_pos.append(act_mark_h + a)
                if h == 0 and prologue_jobs:
                    xt_ready[0] = act_mark_h + a
                ks = gks
                gg = ggs
                for qc in range(NQC):
                    groups = qc_groups[qc]
                    for op in qc_plan[qc][1]:
                        if op[0] == "s":
                            gi = op[1]
                            group = groups[gi]
                            a += 1
                            for kt in group:
                                exp_abs[ks + kt] = act_mark_h + a
                            exp_abs_g[gg + gi] = act_mark_h + a
                        elif op[0] == "tr":
                            a += 1
                            tr_evict_pos.append(act_mark_h + a)
                            kind, a1, a2, n_ = op[1]
                            if kind == "x" and a2 + n_ == NT:
                                xt_ready[a1] = act_mark_h + a
                            if kind == "t" and a2 + n_ == t_tiles[a1][2] // 128:
                                t_ev_end[a1] = act_mark_h + a
                    gg += len(groups)
                    ks += 4 * qc + 4
                for op1 in region_jobs[NQC]:
                    a += 1
                    tr_evict_pos.append(act_mark_h + a)
                    kind, a1, a2, n_ = op1
                    if kind == "x" and a2 + n_ == NT:
                        xt_ready[a1] = act_mark_h + a
                    if kind == "t" and a2 + n_ == t_tiles[a1][2] // 128:
                        t_ev_end[a1] = act_mark_h + a
                n_groups_h = gg - ggs

                dpos = dve_mark_h
                evict_pos = {}
                for qc in range(NQC):
                    for j in range(2):
                        dpos += 1
                        evict_pos[(qc, j)] = dpos
                    for p_ in slot_chunks[qc]:
                        for k in range(9):
                            for i in (2 * p_, 2 * p_ + 1):
                                dpos += 1
                                if k == 0:
                                    t_copy_pos[i] = dpos
                                if k == 8:
                                    t_done_pos[i] = dpos

                @block.tensor
                def _(t, h=h, qc_plan=qc_plan, pe_mark_h=pe_mark_h, gks=gks,
                      ggs=ggs, region_jobs=region_jobs,
                      prologue_jobs=prologue_jobs, gtr0_h=gtr0_h,
                      prev_yev_end=prev_yev_end, evict_pos=evict_pos):
                    if h == 0:
                        barrier_waits(t, "tensor")
                        t.wait_ge(dX[0].h, 16 * 4)
                    g = gtr0_h
                    seen_x = set()
                    seen_t = set()

                    def emit_tr(op1, g):
                        kind, a1, a2, n_ = op1
                        if kind == "x":
                            if a1 not in seen_x:
                                t.wait_ge(dX[a1 % 2].h, 16 * 4 * (a1 // 2 + 1))
                                seen_x.add(a1)
                        else:
                            if a1 not in seen_t:
                                t.wait_ge(sv.h, t_done_pos[a1])
                                seen_t.add(a1)
                        if g >= 2:
                            t.wait_ge(sa.h, tr_evict_pos[g - 2])
                        for k in range(n_):
                            if kind == "x":
                                src = xres[a1 % 2][:, a2 + k, 0:128]
                            else:
                                src = tqt[a1 % 4][:, (a2 + k) * 128:(a2 + k + 1) * 128]
                            ins = t.transpose(ps_tr[g % 2][:, k * 128:(k + 1) * 128],
                                              src, id_bf[:])
                            if k == n_ - 1:
                                sp.inc(ins)
                        assert sp.n == tr_pe_pos[g]

                    for op1 in prologue_jobs:
                        emit_tr(op1, g); g += 1
                    ks = gks
                    gg = ggs
                    for qc in range(NQC):
                        groups, seq, s_end, pv_end_g, pv_lists = qc_plan[qc]

                        def emit_s(gi, ks=ks, qc=qc, gg=gg, groups=groups):
                            gx = gg + gi
                            if gx >= 2:
                                t.wait_ge(sa.h, exp_abs_g[gx - 2])
                            if gi == 0 and qc == 0:
                                t.wait_ge(sa.h, xt_ready[h])
                            diag = len(groups[gi]) == 1 and groups[gi][0] >= 4 * qc
                            for idx, kt in enumerate(groups[gi]):
                                qlo = 128 * max(0, kt - 4 * qc)
                                ins = t.matmul(
                                    ps_s[gx % 2][:, 512 * idx + qlo: 512 * idx + 512],
                                    xT[h % 2][:, kt * 128:(kt + 1) * 128],
                                    xT[h % 2][:, qc * 512 + qlo: qc * 512 + 512],
                                    start=True, stop=not diag)
                                if diag:
                                    ins = t.matmul(
                                        ps_s[gx % 2][:, qlo:qlo + 128],
                                        id_bf[:], mneg[:],
                                        start=False, stop=True)
                                if idx == len(groups[gi]) - 1:
                                    sp.inc(ins)

                        def emit_pv(gi, ks=ks, qc=qc, gg=gg, groups=groups,
                                    pv_lists=pv_lists):
                            gx = gg + gi
                            first = True
                            for idx, kt in enumerate(groups[gi]):
                                kk = ks + kt
                                for u in pv_lists[kt]:
                                    if first:
                                        t.wait_ge(sa.h, exp_abs[kk])
                                        if kt == 0 and qc >= 1:
                                            t.wait_ge(sv.h, evict_pos[(qc - 1, 1)])
                                        if kt == 0 and qc == 0 and prev_yev_end > 0:
                                            t.wait_ge(sv.h, prev_yev_end)
                                        first = False
                                    stop_ = (u % 2 == 1 and kt == 4 * qc + u)
                                    ins = t.matmul(
                                        ps_y[u // 2][:, (u % 2) * 129:(u % 2) * 129 + 129],
                                        expS[gx % 2][:, 512 * idx + u * 128: 512 * idx + (u + 1) * 128],
                                        xres[h % 2][:, kt, 0:129],
                                        start=(kt == 0 and u % 2 == 0),
                                        stop=stop_)
                                    if stop_:
                                        sp.inc(ins)

                        for op in seq:
                            if op[0] == "s":
                                emit_s(op[1])
                            elif op[0] == "pv":
                                emit_pv(op[1])
                            else:
                                emit_tr(op[1], g); g += 1
                        ks += 4 * qc + 4
                        gg += len(groups)
                    for op1 in region_jobs[NQC]:
                        emit_tr(op1, g); g += 1

                @block.scalar
                def _(s, h=h, qc_plan=qc_plan, pe_mark_h=pe_mark_h, gks=gks,
                      ggs=ggs, region_jobs=region_jobs,
                      prologue_jobs=prologue_jobs, gtr0_h=gtr0_h):
                    if h == 0:
                        barrier_waits(s, "scalar")
                    g = gtr0_h
                    seen_t = set()

                    def emit_cp(op1, g):
                        kind, a1, a2, n_ = op1
                        s.wait_ge(sp.h, tr_pe_pos[g])
                        if kind == "x":
                            sa.inc(s.copy(xT[a1 % 2][:, a2 * 128:(a2 + n_) * 128],
                                          ps_tr[g % 2][:, 0:n_ * 128]))
                        else:
                            if a1 not in seen_t and a1 >= 2:
                                s.wait_ge(dSCR[a1 % 2].h, 16 * ((a1 - 2) // 2 + 1))
                            seen_t.add(a1)
                            sa.inc(s.copy(tst[a1 % 2][:, a2:a2 + n_, :],
                                          ps_tr[g % 2][:, 0:n_ * 128]))
                        assert sa.n == tr_evict_pos[g]

                    for op1 in prologue_jobs:
                        emit_cp(op1, g); g += 1
                    ks = gks
                    gg = ggs
                    for qc in range(NQC):
                        groups, seq, s_end, pv_end_g, pv_lists = qc_plan[qc]
                        for op in seq:
                            if op[0] == "s":
                                gi = op[1]
                                group = groups[gi]
                                gx = gg + gi
                                s.wait_ge(sp.h, pe_mark_h + s_end[gi])
                                if len(group) == 2:
                                    sa.inc(s.activation(expS[gx % 2][:, 0:1024],
                                                        ps_s[gx % 2][:, 0:1024],
                                                        AF.Exp, bias=nbias[:],
                                                        scale=EXP_SCALE))
                                else:
                                    kt = group[0]
                                    qlo = 128 * max(0, kt - 4 * qc)
                                    sa.inc(s.activation(expS[gx % 2][:, qlo:512],
                                                        ps_s[gx % 2][:, qlo:512],
                                                        AF.Exp, bias=nbias[:],
                                                        scale=EXP_SCALE))
                                assert sa.n == exp_abs_g[gx]
                            elif op[0] == "tr":
                                emit_cp(op[1], g); g += 1
                        ks += 4 * qc + 4
                        gg += len(groups)
                    for op1 in region_jobs[NQC]:
                        emit_cp(op1, g); g += 1

                @block.vector
                def _(v, h=h, qc_plan=qc_plan, pe_mark_h=pe_mark_h,
                      evict_pos=evict_pos, slot_chunks=slot_chunks):
                    if h == 0:
                        barrier_waits(v, "vector")
                        v.wait_ge(sg.h, INIT_N)
                    for qc in range(NQC):
                        groups, seq, s_end, pv_end_g, pv_lists = qc_plan[qc]
                        for j in range(2):
                            gi_last = groups.index([4 * qc + 2 * j + 1])
                            v.wait_ge(sp.h, pe_mark_h + pv_end_g[gi_last])
                            if j == 0:
                                idx_par = (NQC // 2) * h + qc // 2
                                if idx_par >= 1:
                                    v.wait_ge(dYS[qc % 2].h, 16 * idx_par)
                            sv.inc(v.tensor_copy(
                                ystg[qc % 2][:, 2 * j:2 * j + 2, :],
                                ps_y[j].rearrange("p (a e) -> p a e", e=129)))
                            assert sv.n == evict_pos[(qc, j)]
                        for p_ in slot_chunks[qc]:
                            emit_t_chunk(v, p_)

                @block.gpsimd
                def _(gp, h=h, gks=gks, evict_pos=evict_pos):
                    for qc in range(NQC):
                        gp.wait_ge(sv.h, evict_pos[(qc, 1)])
                        dYS[qc % 2].inc(gp.dma_start(
                            out=ys[h].rearrange("(a p) e -> p a e", p=128)
                            [:, 4 * qc:4 * qc + 4, :],
                            in_=ystg[qc % 2][:, :, :]))

                @block.sync
                def _(sy, h=h):
                    if h == 0:
                        barrier_waits(sy, "sync")
                    hp_ = head_pairs[h]
                    for k2, p_ in enumerate(hp_):
                        emit_t_sync_loads(sy, p_)
                        if k2 >= 1:
                            emit_t_scratch(sy, hp_[k2 - 1])
                    if hp_:
                        emit_t_scratch(sy, hp_[-1])

                gks += sum(4 * qc + 4 for qc in range(NQC))
                ggs += n_groups_h
                pe_head_end.append(sp.n)
                prev_yev_end = evict_pos[(NQC - 1, 1)]

                if h + 2 < H:
                    @block.gpsimd
                    def _(gp, h=h):
                        gp.wait_ge(sp.h, pe_head_end[h])
                        for q, src in enumerate(head_x_src(h + 2)):
                            dX[h % 2].inc(gp.dma_start(
                                out=xres[h % 2][:, q * (NT // 4):(q + 1) * (NT // 4), 0:128],
                                in_=src))

            snap_marks()

        s2 = ExitStack()
        with s2:
            e2 = s2.enter_context
            xb = [e2(nc.sbuf_tensor(f"xb{i}", [128, 4, D], F32)) for i in range(2)]
            yb = e2(nc.sbuf_tensor("yb", [128, 2, H, 129], F32))
            rcp16 = e2(nc.sbuf_tensor("rcp16", [128, H], F32))
            X1T = [e2(nc.sbuf_tensor(f"X1T{i}", [128, NDT, 512], BF16)) for i in range(1)]
            hbuf = e2(nc.sbuf_tensor("hbuf", [128, NHC, 512], BF16))
            W1K = min(7, HID // 128)
            w1b = [e2(nc.sbuf_tensor(f"w1b{i}", [128, D], BF16))
                   for i in range(W1K)]
            w2b = [e2(nc.sbuf_tensor(f"w2b{i}", [128, HID // 4], BF16)) for i in range(2)]
            trel = [e2(nc.sbuf_tensor(f"trel{i}", [128, 512], BF16)) for i in range(2)]
            hsq = trel
            lnt = e2(nc.sbuf_tensor("lnt", [1, 512], F32))
            rsb4 = e2(nc.sbuf_tensor("rsb4", [128, 4], F32))
            yTd = [e2(nc.sbuf_tensor(f"yTd{i}", [128, 512], F32)) for i in range(2)]
            sqb = e2(nc.sbuf_tensor("sqb", [128, D // 4], F32))
            statv = e2(nc.sbuf_tensor("statv", [128, 8], F32))
            s1c = statv[:, 0:1]
            s2c = statv[:, 1:2]
            mc = statv[:, 2:3]
            m2c = statv[:, 3:4]
            vc = statv[:, 4:5]
            rsc = statv[:, 5:6]
            ps_h = [e2(nc.psum_tensor(f"ps_h{i}", [128, 512], F32)) for i in range(2)]
            ps_m = [e2(nc.psum_tensor(f"ps_m{i}", [128, 512], F32)) for i in range(2)]
            ps_den = e2(nc.psum_tensor("ps_den", [128, 512], F32))
            ps_bc = e2(nc.psum_tensor("ps_bc", [128, 512], F32))
            ps_tf = [e2(nc.psum_tensor(f"ps_tf{i}", [128, 128], F32)) for i in range(2)]

            trp_pe = []
            trp_ev = []
            LN_OPS = 17
            CHUNK_OPS = H + 1 + LN_OPS

            def emit_ln(v, st, xap, gate_apply):
                Q4 = D // 4
                st(v.tensor_reduce(s1c[:], xap, AX.X, ALU.add))
                st(v.tensor_scalar(mc[:], s1c[:], 1.0 / D, None, ALU.mult))
                for q4 in range(4):
                    xs = xap[:, q4 * Q4:(q4 + 1) * Q4]
                    st(v.tensor_tensor(sqb[:, :], xs, xs, ALU.mult))
                    st(v.tensor_reduce(m2c[:] if q4 else s2c[:], sqb[:, :],
                                       AX.X, ALU.add))
                    if q4:
                        st(v.tensor_add(s2c[:], s2c[:], m2c[:]))
                st(v.tensor_scalar(s2c[:], s2c[:], 1.0 / D, None, ALU.mult))
                st(v.tensor_tensor(m2c[:], mc[:], mc[:], ALU.mult))
                st(v.tensor_sub(vc[:], s2c[:], m2c[:]))
                gate_apply()
                st(v.tensor_scalar(xap, xap, mc[:], rsc[:],
                                   ALU.subtract, ALU.mult))

            pe_b, sa_b, sv_b, dxy_b = sp.n, sa.n, sv.n, dXY.n

            def bp0(tp, j, sv_b=sv_b):
                return sv_b + tp * CHUNK_OPS + j

            @block.sync
            def _(sy):
                barrier_waits(sy, "sync")
                sy.wait_ge(dYS[0].h, 16 * (NQC // 2) * H)
                sy.wait_ge(dYS[1].h, 16 * (NQC // 2) * H)
                sy.wait_ge(dSCR[0].h, 16 * ((NTT + 1) // 2))
                sy.wait_ge(dSCR[1].h, 16 * (NTT // 2))
                dXY.inc(sy.dma_start(
                    out=xb[0][:, :, :],
                    in_=x_hbm[0:512, :].rearrange("(a p) d -> p a d", p=128)))
                for tp in range(4):
                    if tp == 2:
                        sy.wait_ge(sv.h, bp0(1, H + 1))
                    for h in range(H):
                        dXY.inc(sy.dma_start(
                            out=yb[:, tp % 2, h, :],
                            in_=ys[h].rearrange("(a p) e -> p a e", p=128)[:, tp, :]))

            @block.vector
            def _(v):
                def st(ins):
                    sv.inc(ins)
                    v.wait_ge(sv.h, sv.n)
                barrier_waits(v, "vector")
                for tp in range(4):
                    v.wait_ge(dXY.h, dxy_b + 16 * (1 + (2 if tp < 2 else 4) * H))
                    st(v.reciprocal(rcp16[:, :], yb[:, tp % 2, :, 128]))
                    for h in range(H):
                        st(v.scalar_tensor_tensor(
                            xb[0][:, tp, h * 128:(h + 1) * 128],
                            yb[:, tp % 2, h, 0:128],
                            rcp16[:, h:h + 1],
                            xb[0][:, tp, h * 128:(h + 1) * 128],
                            ALU.mult, ALU.add))
                    emit_ln(v, st, xb[0][:, tp, :],
                            lambda tp=tp: v.wait_ge(sa.h, sa_b + 2 * tp + 2))

            b0_stt_end = [bp0(tp, H + 1) for tp in range(4)]

            @block.scalar
            def _(s):
                def st(ins):
                    sa.inc(ins)
                    s.wait_ge(sa.h, sa.n)
                barrier_waits(s, "scalar")
                for tp in range(4):
                    s.wait_ge(sv.h, bp0(tp, H + LN_OPS))
                    st(s.activation(vc[:], vc[:], AF.Ln, bias=eps_ln[:]))
                    st(s.activation(rsc[:], vc[:], AF.Exp, scale=-0.5))

            pe_bT = sp.n
            for g in range(4 * NDT):
                trp_pe.append(pe_bT + g + 1)
                trp_ev.append(("sa", sa.n + g + 1))

            @block.tensor
            def _(t):
                barrier_waits(t, "tensor")
                g = 0
                for tq in range(4):
                    for dt in range(NDT):
                        if dt == 0:
                            t.wait_ge(sv.h, bp0(tq, H + 1 + LN_OPS))
                        if g >= 2:
                            k_, n_ = trp_ev[g - 2]
                            t.wait_ge(sa.h if k_ == "sa" else sv.h, n_)
                        sp.inc(t.transpose(ps_tf[g % 2][:],
                                           xb[0][:, tq, dt * 128:(dt + 1) * 128],
                                           id_f[:]))
                        assert sp.n == trp_pe[g]
                        g += 1

            @block.scalar
            def _(s):
                g = 0
                for tq in range(4):
                    for dt in range(NDT):
                        s.wait_ge(sp.h, trp_pe[g])
                        sa.inc(s.copy(X1T[0][:, dt, tq * 128:(tq + 1) * 128],
                                      ps_tf[g % 2][:]))
                        assert ("sa", sa.n) == trp_ev[g]
                        g += 1

            gtb = 4 * NDT
            x1t_ready = sa.n
            prev = {"b_stt_end3": b0_stt_end[3]}
            snap_marks()

            for c in range(NQC):
                pe_c, sa_c, sv_c, dxy_c = sp.n, sa.n, sv.n, dXY2.n
                dw1_c = [c_.n for c_ in dW1]
                dw2_c = [c_.n for c_ in dW2]
                has_b = c + 1 < NQC
                nxt = (c + 1) % 2

                ch_ins = {(2 * tp + 1) * NHC // 8: tp for tp in range(4)} if has_b else {}
                bt_ins = {}
                if has_b:
                    for i_ in range(4):
                        bt_ins.setdefault(max(0, (i_ + 1) * NDT // 4 - 1), []).append(i_)

                p = 0
                m1_end = {}
                den_pos = {}
                m2_pos = {}
                m2_end = {}
                ctr_pe = {}
                gt0_c = gtb
                gtw = gtb
                bt_pe = {}
                for hc in range(NHC):
                    p += 1; m1_end[hc] = p
                    if hc >= 1:
                        p += 1; den_pos[hc - 1] = p
                p += 1; den_pos[NHC - 1] = p
                p += 1; bcast_pos = p
                m2_q = {}
                m2_jobs = {}
                for dt in range(NDT):
                    jl = []
                    if dt >= 2:
                        jl += [("c", dt - 2, tq) for tq in range(4)]
                    for i_ in bt_ins.get(dt, []):
                        jl += [("b", NDT * i_ + j_, 0) for j_ in range(NDT)]
                    if dt == NDT - 1:
                        jl += [("c", NDT - 2, tq) for tq in range(4)]
                    m2_jobs[dt] = jl
                m2_seq = {}
                for dt in range(NDT):
                    jl = m2_jobs[dt]
                    span = max(NHC - 4, 1)
                    ins_after = [0] * NHC
                    for x in range(len(jl)):
                        ins_after[min((x * span) // max(len(jl), 1), NHC - 5)] += 1
                    seq = []
                    ji = 0
                    for hc in range(NHC):
                        seq.append(("mm", hc))
                        for _ in range(ins_after[hc]):
                            seq.append(("job", jl[ji])); ji += 1
                    assert ji == len(jl)
                    if dt == NDT - 1:
                        seq += [("job", ("c", NDT - 1, tq)) for tq in range(4)]
                    m2_seq[dt] = seq
                    m2_pos[dt] = p
                    qn = 0
                    for op in seq:
                        if op[0] == "mm":
                            if (op[1] + 1) % (NHC // 4) == 0:
                                p += 1
                                m2_q[(dt, qn)] = p
                                qn += 1
                                if qn == 4:
                                    m2_end[dt] = p
                        else:
                            kindc, a1, a2 = op[1]
                            p += 1
                            trp_pe.append(pe_c + p)
                            if kindc == "c":
                                ctr_pe[(a1, a2)] = gtw
                            else:
                                bt_pe[a1] = gtw
                            gtw += 1
                pe_total_c = p

                d = 0
                d_tt = {}
                d_hsq = {}
                b_var = {}
                b_apply = {}
                b_stt_end = {}
                for hc in range(NHC):
                    d += 1; d_tt[hc] = sv_c + d
                    d += 1; d_hsq[hc] = sv_c + d
                    if hc in ch_ins:
                        tp = ch_ins[hc]
                        b_stt_end[tp] = sv_c + d + 1 + H
                        b_var[tp] = sv_c + d + H + LN_OPS
                        b_apply[tp] = sv_c + d + CHUNK_OPS
                        d += CHUNK_OPS
                d += 1; d_bcev = sv_c + d
                d_ev = {}
                d_add = {}
                for dt in range(NDT):
                    d += 1; d_ev[dt] = sv_c + d
                    for tq in range(4):
                        d += 1; d_add[(dt, tq)] = sv_c + d
                d_ln2 = {}
                for tq in range(4):
                    for j in range(1, LN_OPS + 1):
                        d += 1; d_ln2[(tq, j)] = sv_c + d

                a = 0
                a_relu = {}
                a_bln = {}
                for hc in range(NHC):
                    a += 1; a_relu[hc] = sa_c + a
                    if hc in ch_ins:
                        a += 2; a_bln[ch_ins[hc]] = sa_c + a
                a += 2; a_rms = sa_c + a
                a_xev = {}
                if has_b:
                    for g in range(4 * NDT):
                        a += 1; a_xev[g] = sa_c + a
                a_cln = {}
                for tq in range(4):
                    a += 2; a_cln[tq] = sa_c + a
                tmp_ev = {}
                for (dtc, tq), gti in ctr_pe.items():
                    tmp_ev[gti] = ("sv", d_add[(dtc, tq)])
                for jobi, gti in bt_pe.items():
                    tmp_ev[gti] = ("sa", a_xev[jobi])
                while len(trp_ev) < gtw:
                    trp_ev.append(None)
                for gti in range(gt0_c, gtw):
                    trp_ev[gti] = tmp_ev[gti]
                gtb = gtw

                @block.gpsimd
                def _(gp, c=c, has_b=has_b, nxt=nxt, dxy_c=dxy_c, prev=prev,
                      d_ln2=d_ln2, b_stt_end=b_stt_end):
                    if has_b:
                        if c >= 1:
                            gp.wait_ge(dOUT.h, 16 * c)
                        gp.wait_ge(sv.h, prev["b_stt_end3"])
                        dXY2.inc(gp.dma_start(
                            out=xb[nxt][:, :, :],
                            in_=x_hbm[(c + 1) * 512:(c + 2) * 512, :]
                            .rearrange("(a p) d -> p a d", p=128)))
                        for tp in range(2):
                            for h in range(H):
                                dXY2.inc(gp.dma_start(
                                    out=yb[:, tp % 2, h, :],
                                    in_=ys[h].rearrange("(a p) e -> p a e", p=128)
                                    [:, 4 * (c + 1) + tp, :]))
                        gp.wait_ge(sv.h, b_stt_end[1])
                        for tp in range(2, 4):
                            for h in range(H):
                                dXY2.inc(gp.dma_start(
                                    out=yb[:, tp % 2, h, :],
                                    in_=ys[h].rearrange("(a p) e -> p a e", p=128)
                                    [:, 4 * (c + 1) + tp, :]))
                    gp.wait_ge(sv.h, d_ln2[(3, LN_OPS)])
                    dOUT.inc(gp.dma_start(
                        out=out_hbm[c * 512:(c + 1) * 512, :]
                        .rearrange("(a p) d -> p a d", p=128),
                        in_=xb[c % 2][:, :, :]))

                @block.sync
                def _(sy, c=c, prev=prev, has_b=has_b, m1_end=m1_end,
                      m2_q=m2_q):
                    if c == 0:
                        barrier_waits(sy, "sync")
                        sy.wait_ge(dSCR[0].h, 16 * ((NTT + 1) // 2))
                        sy.wait_ge(dSCR[1].h, 16 * (NTT // 2))
                    items = []
                    if c == 0:
                        for j in range(W1K):
                            items.append((-1, 0, j))
                    for j in range(c * NHC + W1K, (c + 1) * NHC + W1K):
                        if j >= (c + 1) * NHC and not has_b:
                            break
                        items.append((pe_c + m1_end[j - W1K - c * NHC], 0, j))
                    for dt in range(NDT):
                        for q_ in range(4):
                            if q_ >= 2:
                                g_ = pe_c + m2_q[(dt, q_ - 2)]
                            elif dt >= 1:
                                g_ = pe_c + m2_q[(dt - 1, min(q_ + 2, 3))]
                            elif c >= 1:
                                g_ = prev["m2_q15"][q_]
                            else:
                                g_ = 0
                            items.append((g_, 1, (dt, q_)))
                    for g_, kind_, a_ in sorted(items, key=lambda x: (x[0], x[1] == 0)):
                        if g_ > 0:
                            sy.wait_ge(sp.h, g_)
                        if kind_ == 0:
                            j = a_
                            dW1[j % W1K].inc(sy.dma_start(
                                out=w1b[j % W1K][:, :], in_=w1t[j % NHC, :, :]))
                        else:
                            dt, q_ = a_
                            dW2[q_].inc(sy.dma_start(
                                out=w2b[q_ % 2][:, :],
                                in_=w2t[dt, :, q_ * (HID // 4):(q_ + 1) * (HID // 4)]))

                @block.tensor
                def _(t, c=c, has_b=has_b, nxt=nxt, prev=prev, dw1_c=dw1_c,
                      dw2_c=dw2_c, m1_end=m1_end, den_pos=den_pos,
                      bcast_pos=bcast_pos, m2_pos=m2_pos, m2_end=m2_end,
                      d_hsq=d_hsq, d_ev=d_ev, b_apply=b_apply, a_relu=a_relu,
                      a_rms=a_rms, ch_ins=ch_ins, bt_ins=bt_ins,
                      ctr_pe=ctr_pe, bt_pe=bt_pe, gt0_c=gt0_c,
                      m2_seq=m2_seq):
                    if c == 0:
                        barrier_waits(t, "tensor")
                        t.wait_ge(sa.h, x1t_ready)
                    else:
                        t.wait_ge(sa.h, prev["x1t_ready"])
                    seen_bq = set()
                    for hc in range(NHC):
                        jg = c * NHC + hc
                        t.wait_ge(dW1[jg % W1K].h, 16 * (jg // W1K + 1))
                        if hc >= 2:
                            t.wait_ge(sa.h, a_relu[hc - 2])
                        elif c >= 1:
                            t.wait_ge(sa.h, prev["a_relu"][NHC - 2 + hc])
                        for dt in range(NDT):
                            ins = t.matmul(ps_h[hc % 2][:, :],
                                           w1b[(c * NHC + hc) % W1K][:, dt * 128:(dt + 1) * 128],
                                           X1T[0][:, dt, :],
                                           start=(dt == 0), stop=(dt == NDT - 1))
                            if dt == NDT - 1:
                                sp.inc(ins)
                        if hc >= 1:
                            dn = hc - 1
                            t.wait_ge(sv.h, d_hsq[dn])
                            if dn == 0 and c >= 1:
                                t.wait_ge(sa.h, prev["a_rms"])
                            sp.inc(t.matmul(ps_den[0:1, :], ones_bf[:],
                                            hsq[dn % 2][:, :],
                                            start=(dn == 0), stop=(dn == NHC - 1)))
                        assert sp.n == pe_c + m1_end[hc] + (1 if hc >= 1 else 0)
                    t.wait_ge(sv.h, d_hsq[NHC - 1])
                    sp.inc(t.matmul(ps_den[0:1, :], ones_bf[:],
                                    hsq[(NHC - 1) % 2][:, :],
                                    start=False, stop=True))
                    t.wait_ge(sa.h, a_rms)
                    if c >= 1:
                        t.wait_ge(sv.h, prev["d_bcev"])
                    for tq in range(4):
                        ins = t.matmul(ps_bc[:, tq:tq + 1],
                                       lnt[0:1, tq * 128:(tq + 1) * 128],
                                       ones_f[0:1, 0:1],
                                       start=(tq == 0), stop=(tq == 3))
                        if tq == 3:
                            sp.inc(ins)
                    assert sp.n == pe_c + bcast_pos
                    seen_ct = set()
                    for dt in range(NDT):
                        for op in m2_seq[dt]:
                            if op[0] == "mm":
                                hc = op[1]
                                q_ = hc // (NHC // 4)
                                off = hc - q_ * (NHC // 4)
                                if off == 0:
                                    t.wait_ge(dW2[q_].h, dw2_c[q_] + 16 * (dt + 1))
                                if hc == 0:
                                    if dt >= 2:
                                        t.wait_ge(sv.h, d_ev[dt - 2])
                                    elif c >= 1:
                                        t.wait_ge(sv.h, prev["d_ev"][NDT - 2 + dt])
                                ins = t.matmul(
                                    ps_m[dt % 2][:, :],
                                    w2b[q_ % 2][:, off * 128:(off + 1) * 128],
                                    hbuf[:, hc, :],
                                    start=(hc == 0), stop=(hc == NHC - 1))
                                if (hc + 1) % (NHC // 4) == 0:
                                    sp.inc(ins)
                            else:
                                kindc, a1, a2 = op[1]
                                if kindc == "c":
                                    gti = ctr_pe[(a1, a2)]
                                    if a1 not in seen_ct:
                                        t.wait_ge(sv.h, d_ev[a1])
                                        seen_ct.add(a1)
                                    src = yTd[a1 % 2][:, a2 * 128:(a2 + 1) * 128]
                                else:
                                    gti = bt_pe[a1]
                                    tq, dtp = a1 // NDT, a1 % NDT
                                    if tq not in seen_bq:
                                        t.wait_ge(sv.h, b_apply[tq])
                                        seen_bq.add(tq)
                                    src = xb[nxt][:, tq, dtp * 128:(dtp + 1) * 128]
                                if gti >= 2:
                                    k_, n_ = trp_ev[gti - 2]
                                    t.wait_ge(sa.h if k_ == "sa" else sv.h, n_)
                                sp.inc(t.transpose(ps_tf[gti % 2][:], src, id_f[:]))
                                assert sp.n == trp_pe[gti]

                @block.scalar
                def _(s, c=c, has_b=has_b, nxt=nxt, m1_end=m1_end,
                      den_pos=den_pos, d_tt=d_tt, b_var=b_var, ch_ins=ch_ins,
                      d_ln2=d_ln2, bt_pe=bt_pe, a_xev=a_xev):
                    def st(ins):
                        sa.inc(ins)
                        s.wait_ge(sa.h, sa.n)

                    if c == 0:
                        barrier_waits(s, "scalar")
                    for hc in range(NHC):
                        s.wait_ge(sp.h, pe_c + m1_end[hc])
                        if hc >= 2:
                            s.wait_ge(sv.h, d_hsq[hc - 2])
                        sa.inc(s.activation(trel[hc % 2][:, :], ps_h[hc % 2][:, :],
                                            AF.Relu))
                        if hc in ch_ins:
                            tp = ch_ins[hc]
                            s.wait_ge(sv.h, b_var[tp])
                            st(s.activation(vc[:], vc[:], AF.Ln, bias=eps_ln[:]))
                            st(s.activation(rsc[:], vc[:], AF.Exp, scale=-0.5))
                    s.wait_ge(sp.h, pe_c + den_pos[NHC - 1])
                    st(s.activation(lnt[:, :], ps_den[0:1, :], AF.Ln,
                                    bias=eps_rms[0:1, :], scale=1.0 / HID))
                    st(s.activation(lnt[:, :], lnt[:, :], AF.Exp, scale=-0.5))
                    if has_b:
                        for jobi in range(4 * NDT):
                            tq, dtp = jobi // NDT, jobi % NDT
                            gti = bt_pe[jobi]
                            s.wait_ge(sp.h, trp_pe[gti])
                            sa.inc(s.copy(X1T[0][:, dtp, tq * 128:(tq + 1) * 128],
                                          ps_tf[gti % 2][:]))
                            assert ("sa", sa.n) == trp_ev[gti]
                    for tq in range(4):
                        s.wait_ge(sv.h, d_ln2[(tq, LN_OPS - 1)])
                        st(s.activation(vc[:], vc[:], AF.Ln, bias=eps_ln[:]))
                        st(s.activation(rsc[:], vc[:], AF.Exp, scale=-0.5))

                @block.vector
                def _(v, c=c, has_b=has_b, nxt=nxt, dxy_c=dxy_c, a_relu=a_relu,
                      den_pos=den_pos, bcast_pos=bcast_pos, m2_end=m2_end,
                      ctr_pe=ctr_pe, ch_ins=ch_ins, a_bln=a_bln, a_cln=a_cln,
                      prev=prev, d_ev=d_ev, d_add=d_add):
                    def st(ins):
                        sv.inc(ins)
                        v.wait_ge(sv.h, sv.n)
                    if c == 0:
                        barrier_waits(v, "vector")
                    for hc in range(NHC):
                        v.wait_ge(sa.h, a_relu[hc])
                        if hc >= 2:
                            v.wait_ge(sp.h, pe_c + den_pos[hc - 2])
                        elif c >= 1:
                            v.wait_ge(sp.h, prev["pe_c"] + prev["den_pos"][NHC - 2 + hc])
                        st(v.tensor_tensor(hbuf[:, hc, :], trel[hc % 2][:, :],
                                           trel[hc % 2][:, :], ALU.mult))
                        st(v.tensor_tensor(hsq[hc % 2][:, :], hbuf[:, hc, :],
                                           hbuf[:, hc, :], ALU.mult))
                        if hc in ch_ins:
                            tp = ch_ins[hc]
                            v.wait_ge(dXY2.h, dxy_c + 16 * (1 + (2 if tp < 2 else 4) * H))
                            st(v.reciprocal(rcp16[:, :], yb[:, tp % 2, :, 128]))
                            for h in range(H):
                                st(v.scalar_tensor_tensor(
                                    xb[nxt][:, tp, h * 128:(h + 1) * 128],
                                    yb[:, tp % 2, h, 0:128],
                                    rcp16[:, h:h + 1],
                                    xb[nxt][:, tp, h * 128:(h + 1) * 128],
                                    ALU.mult, ALU.add))
                            emit_ln(v, st, xb[nxt][:, tp, :],
                                    lambda tp=tp: v.wait_ge(sa.h, a_bln[tp]))
                            assert sv.n == b_apply[tp]
                    v.wait_ge(sp.h, pe_c + bcast_pos)
                    st(v.tensor_copy(rsb4[:, :], ps_bc[:, 0:4]))
                    assert sv.n == d_bcev
                    for dt in range(NDT):
                        v.wait_ge(sp.h, pe_c + m2_end[dt])
                        if dt >= 2:
                            v.wait_ge(sp.h, trp_pe[ctr_pe[(dt - 2, 3)]])
                        elif c >= 1:
                            v.wait_ge(sp.h, prev["ctr_last_pe"][dt])
                        st(v.tensor_copy(yTd[dt % 2][:, :], ps_m[dt % 2][:, :]))
                        assert sv.n == d_ev[dt]
                        for tq in range(4):
                            v.wait_ge(sp.h, trp_pe[ctr_pe[(dt, tq)]])
                            st(v.scalar_tensor_tensor(
                                xb[c % 2][:, tq, dt * 128:(dt + 1) * 128],
                                ps_tf[ctr_pe[(dt, tq)] % 2][:],
                                rsb4[:, tq:tq + 1],
                                xb[c % 2][:, tq, dt * 128:(dt + 1) * 128],
                                ALU.mult, ALU.add))
                    for tq in range(4):
                        emit_ln(v, st, xb[c % 2][:, tq, :],
                                lambda tq=tq: v.wait_ge(sa.h, a_cln[tq]))
                        assert sv.n == d_ln2[(tq, LN_OPS)]

                prev = {
                    "b_stt_end3": b_stt_end.get(3, prev.get("b_stt_end3", 0)),
                    "x1t_ready": (a_xev[4 * NDT - 1] if has_b else 0),
                    "a_relu": a_relu,
                    "a_rms": a_rms,
                    "d_bcev": d_bcev,
                    "d_ev": d_ev,
                    "pe_c": pe_c,
                    "den_pos": den_pos,
                    "ctr_last_pe": {dtx: trp_pe[ctr_pe[(NDT - 2 + dtx, 3)]]
                                    for dtx in range(2)},
                    "m2_q15": [pe_c + m2_q[(NDT - 1, min(q_ + 2, 3))]
                               for q_ in range(2)],
                }

        @block.gpsimd
        def _(gp):
            gp.wait_ge(dOUT.h, 16 * NQC)

    return nc


def kernel(L0_attn_x, L0_ffn_W_fc, L0_ffn_W_proj):
    x = np.ascontiguousarray(np.asarray(L0_attn_x, dtype=np.float32))
    wfc = np.ascontiguousarray(np.asarray(L0_ffn_W_fc, dtype=np.float32))
    wproj = np.ascontiguousarray(np.asarray(L0_ffn_W_proj, dtype=np.float32))
    B = x.shape[0]
    nc = build_program(L=x.shape[1], D=x.shape[2], HID=wfc.shape[0])
    in_maps = [{"x": x[b], "wfc": wfc, "wproj": wproj} for b in range(B)]
    res = run_bass_kernel_spmd(nc, in_maps, core_ids=list(range(B)))
    return np.stack([res.results[b]["out"] for b in range(B)], axis=0)


# version 23
# speedup vs baseline: 1.2260x; 1.0629x over previous
import numpy as np
from contextlib import ExitStack

import concourse.bass as bass
import concourse.mybir as mybir
from concourse.bass_utils import run_bass_kernel_spmd

F32 = mybir.dt.float32
BF16 = mybir.dt.bfloat16
AF = mybir.ActivationFunctionType
ALU = mybir.AluOpType
AX = mybir.AxisListType

EXP_SCALE = float(np.float32(1.0) / np.sqrt(np.float32(128.0)))
EXP_SHIFT = -10.0
EPS_LN = 1e-5
EPS_RMS = float(np.finfo(np.float32).eps)
G = 64


class Ctr:
    def __init__(self, sem, step=1):
        self.h = sem
        self.n = 0
        self.step = step

    def inc(self, ins):
        ins.then_inc(self.h, self.step)
        self.n += self.step
        return self.n


def build_program(L=4096, D=2048, HID=8192):
    H = D // 128
    NT = L // 128
    NQC = L // 512
    NDT = D // 128
    NHC = HID // 128
    W2F = min(2048, HID)
    NW2C = HID // W2F
    NT1 = HID // 128
    NTT = NT1 + NDT * NW2C
    assert L % 512 == 0 and D % 128 == 0 and HID % 128 == 0 and NT % 4 == 0

    nc = bass.Bass("TRN2", target_bir_lowering=False)
    x_hbm = nc.dram_tensor("x", [L, D], F32, kind="ExternalInput").ap()
    wfc = nc.dram_tensor("wfc", [HID, D], F32, kind="ExternalInput").ap()
    wproj = nc.dram_tensor("wproj", [D, HID], F32, kind="ExternalInput").ap()
    out_hbm = nc.dram_tensor("out", [L, D], F32, kind="ExternalOutput").ap()
    w1t = nc.dram_tensor("w1t", [NHC, 128, D], BF16, kind="Internal").ap()
    w2t = nc.dram_tensor("w2t", [NDT, 128, HID], BF16, kind="Internal").ap()
    ys = nc.dram_tensor("ys", [H, L, 129], F32, kind="Internal").ap()

    ctx = ExitStack()
    with ctx:
        ec = ctx.enter_context

        id_bf = ec(nc.sbuf_tensor("id_bf", [128, 128], BF16))
        id_f = ec(nc.sbuf_tensor("id_f", [128, 128], F32))
        tri = ec(nc.sbuf_tensor("tri", [128, 128], BF16))
        mneg = ec(nc.sbuf_tensor("mneg", [128, 128], BF16))
        nbias = ec(nc.sbuf_tensor("nbias", [128, 1], F32))
        eps_ln = ec(nc.sbuf_tensor("eps_ln", [128, 1], F32))
        eps_rms = ec(nc.sbuf_tensor("eps_rms", [128, 1], F32))
        ones_bf = ec(nc.sbuf_tensor("ones_bf", [128, 1], BF16))
        ones_f = ec(nc.sbuf_tensor("ones_f", [1, 32], F32))

        sp = Ctr(ec(nc.semaphore("sp")))
        sa = Ctr(ec(nc.semaphore("sa")))
        sv = Ctr(ec(nc.semaphore("sv")))
        sg = Ctr(ec(nc.semaphore("sg")))
        dW = [Ctr(ec(nc.semaphore(f"dW{i}")), 16) for i in range(2)]
        dSCR = [Ctr(ec(nc.semaphore(f"dSCR{i}")), 16) for i in range(2)]
        dX = [Ctr(ec(nc.semaphore(f"dX{i}")), 16) for i in range(2)]
        dYS = [Ctr(ec(nc.semaphore(f"dYS{i}")), 16) for i in range(2)]
        dXY = Ctr(ec(nc.semaphore("dXY")), 16)
        dXY2 = Ctr(ec(nc.semaphore("dXY2")), 16)
        dW1 = [Ctr(ec(nc.semaphore(f"dW1_{i}")), 16) for i in range(8)]
        dW2 = [Ctr(ec(nc.semaphore(f"dW2_{i}")), 16) for i in range(4)]
        dOUT = Ctr(ec(nc.semaphore("dOUT")), 16)

        block = ec(nc.Block())

        chain = {"tensor": sp, "vector": sv, "scalar": sa, "gpsimd": sg}
        marks = {e: 0 for e in chain}

        def snap_marks():
            for e in chain:
                marks[e] = chain[e].n

        def barrier_waits(eng, me):
            for e, c in chain.items():
                if e != me and marks[e] > 0:
                    eng.wait_ge(c.h, marks[e])

        @block.gpsimd
        def _(gp):
            def st(ins):
                sg.inc(ins)
                gp.wait_ge(sg.h, sg.n)
            st(gp.memset(id_bf[:], 0.0))
            st(gp.affine_select(out=id_bf[:], in_=id_bf[:], compare_op=ALU.not_equal,
                                fill=1.0, base=0, pattern=[[-1, 128]], channel_multiplier=1))
            st(gp.memset(id_f[:], 0.0))
            st(gp.affine_select(out=id_f[:], in_=id_f[:], compare_op=ALU.not_equal,
                                fill=1.0, base=0, pattern=[[-1, 128]], channel_multiplier=1))
            st(gp.memset(tri[:], 0.0))
            st(gp.affine_select(out=tri[:], in_=tri[:], compare_op=ALU.is_gt,
                                fill=1.0, base=0, pattern=[[-1, 128]], channel_multiplier=1))
            st(gp.memset(mneg[:], -3000.0))
            st(gp.affine_select(out=mneg[:], in_=mneg[:], compare_op=ALU.is_gt,
                                fill=0.0, base=0, pattern=[[-1, 128]], channel_multiplier=1))
            sg.inc(gp.memset(nbias[:], EXP_SHIFT))
            sg.inc(gp.memset(eps_ln[:], EPS_LN))
            sg.inc(gp.memset(eps_rms[:], EPS_RMS))
            sg.inc(gp.memset(ones_bf[:], 1.0))
            sg.inc(gp.memset(ones_f[:], 1.0))

        INIT_N0 = sg.n

        t_tiles = []
        for i in range(NT1):
            t_tiles.append((wfc[i * 128:(i + 1) * 128, :], w1t[i, :, :], D))
        for dt in range(NDT):
            for c in range(NW2C):
                t_tiles.append((wproj[dt * 128:(dt + 1) * 128, c * W2F:(c + 1) * W2F],
                                w2t[dt, :, c * W2F:(c + 1) * W2F], W2F))
        assert len(t_tiles) == NTT

        s1 = ExitStack()
        with s1:
            e1 = s1.enter_context
            wtf = [e1(nc.sbuf_tensor(f"wtf{i}", [128, 2048], F32)) for i in range(2)]
            wbb = [e1(nc.sbuf_tensor(f"wbb{i}", [128, 2048], BF16)) for i in range(2)]
            tgs = [e1(nc.sbuf_tensor(f"tgs{i}", [128, 2048 // G], F32)) for i in range(2)]
            tsc = [e1(nc.sbuf_tensor(f"tsc{i}", [128, 2048 // G], BF16)) for i in range(2)]
            trc = [e1(nc.sbuf_tensor(f"trc{i}", [128, 2048 // G], F32)) for i in range(2)]
            trr = [e1(nc.sbuf_tensor(f"trr{i}", [128, 2048], BF16)) for i in range(2)]
            tg1 = [e1(nc.sbuf_tensor(f"tg1{i}", [128, 2048], BF16)) for i in range(2)]
            tg2 = [e1(nc.sbuf_tensor(f"tg2{i}", [128, 2048], BF16)) for i in range(2)]
            tqt = [e1(nc.sbuf_tensor(f"tqt{i}", [128, 2048], BF16)) for i in range(4)]
            tst = [e1(nc.sbuf_tensor(f"tst{i}", [128, 16, 128], BF16,
                                       is_transpose_dest=True)) for i in range(2)]
            xres = [e1(nc.sbuf_tensor(f"xres{i}", [128, NT, 129], BF16)) for i in range(2)]
            xT = [e1(nc.sbuf_tensor(f"xT{i}", [128, L], BF16)) for i in range(2)]
            expS = [e1(nc.sbuf_tensor(f"expS{i}", [128, 1024], BF16)) for i in range(2)]
            ystg = [e1(nc.sbuf_tensor(f"ystg{i}", [128, 4, 129], F32)) for i in range(2)]
            ps_tr = [e1(nc.psum_tensor(f"ps_tr{i}", [128, 512], BF16)) for i in range(2)]
            ps_s = [e1(nc.psum_tensor(f"ps_s{i}", [128, 1024], F32)) for i in range(2)]
            ps_y = [e1(nc.psum_tensor(f"ps_y{i}", [128, 258], F32)) for i in range(2)]

            @block.gpsimd
            def _(gp):
                sg.inc(gp.memset(xres[0][:, :, 128:129], 1.0))
                sg.inc(gp.memset(xres[1][:, :, 128:129], 1.0))

            INIT_N = sg.n

            assert NTT % 2 == 0
            n_pairs = NTT // 2
            head_pairs = [[] for _ in range(H)]
            for p_ in range(n_pairs):
                head_pairs[(p_ * H) // n_pairs].append(p_)

            t_copy_pos = {}
            t_done_pos = {}
            t_ev_end = {}
            t_pe_end = {}
            tr_evict_pos = []
            tr_pe_pos = []

            def t_dve_ops(i, F):
                b = i % 2
                b4 = i % 4
                ng = F // G
                return [
                    lambda v: v.tensor_copy(wbb[b][:, 0:F], wtf[b][:, 0:F]),
                    lambda v: v.tensor_reduce(
                        tgs[b].rearrange("p (g o) -> p g o", o=1)[:, 0:ng, :],
                        wbb[b].rearrange("p (g e) -> p g e", e=G)[:, 0:ng, :],
                        AX.X, ALU.add, apply_absolute_value=True),
                    lambda v: v.tensor_scalar(tsc[b][:, 0:ng], tgs[b][:, 0:ng],
                                              1.0 / G, 1e-8, ALU.mult, ALU.max),
                    lambda v: v.reciprocal(trc[b][:, 0:ng], tsc[b][:, 0:ng]),
                    lambda v: v.tensor_tensor(
                        trr[b].rearrange("p (g e) -> p g e", e=G)[:, 0:ng, :],
                        wbb[b].rearrange("p (g e) -> p g e", e=G)[:, 0:ng, :],
                        trc[b].rearrange("p (g o) -> p g o", o=1)[:, 0:ng, :]
                           .broadcast_to([128, ng, G]),
                        ALU.mult),
                    lambda v: v.tensor_scalar(tg1[b][:, 0:F], trr[b][:, 0:F],
                                              0.5, None, ALU.is_gt),
                    lambda v: v.tensor_scalar(tg2[b][:, 0:F], trr[b][:, 0:F],
                                              -0.5, None, ALU.is_lt),
                    lambda v: v.tensor_sub(tqt[b4][:, 0:F], tg1[b][:, 0:F],
                                           tg2[b][:, 0:F]),
                    lambda v: v.tensor_tensor(
                        tqt[b4].rearrange("p (g e) -> p g e", e=G)[:, 0:ng, :],
                        tqt[b4].rearrange("p (g e) -> p g e", e=G)[:, 0:ng, :],
                        tsc[b].rearrange("p (g o) -> p g o", o=1)[:, 0:ng, :]
                           .broadcast_to([128, ng, G]),
                        ALU.mult),
                ]

            def emit_t_chunk(v, p_):
                ia, ib = 2 * p_, 2 * p_ + 1
                opsa = t_dve_ops(ia, t_tiles[ia][2])
                opsb = t_dve_ops(ib, t_tiles[ib][2])
                v.wait_ge(dW[0].h, 16 * (p_ + 1))
                v.wait_ge(dW[1].h, 16 * (p_ + 1))
                for k in range(9):
                    for i, ops in ((ia, opsa), (ib, opsb)):
                        if k > 0:
                            v.wait_ge(sv.h, sv.n - 1)
                        if k == 7 and i >= 4:
                            v.wait_ge(sp.h, t_pe_end[i - 4])
                        sv.inc(ops[k](v))
                        if k == 0:
                            assert sv.n == t_copy_pos[i]
                        if k == 8:
                            assert sv.n == t_done_pos[i]

            def emit_t_sync_loads(sy, p_):
                for i in (2 * p_, 2 * p_ + 1):
                    if i >= 2:
                        sy.wait_ge(sv.h, t_copy_pos[i - 2])
                    dW[i % 2].inc(sy.dma_start(out=wtf[i % 2][:, 0:t_tiles[i][2]],
                                               in_=t_tiles[i][0]))

            def emit_t_scratch(q, p_):
                for i in (2 * p_, 2 * p_ + 1):
                    F = t_tiles[i][2]
                    q.wait_ge(sa.h, t_ev_end[i])
                    dSCR[i % 2].inc(q.dma_start(out=t_tiles[i][1],
                                                in_=tst[i % 2][:, 0:F // 128, :]))

            def head_x_src(h):
                full = x_hbm[:, h * 128:(h + 1) * 128].rearrange("(a p) d -> p a d", p=128)
                return [full[:, q * (NT // 4):(q + 1) * (NT // 4), :] for q in range(4)]

            @block.gpsimd
            def _(gp):
                barrier_waits(gp, "gpsimd")
                for h in range(min(2, H)):
                    for q, src in enumerate(head_x_src(h)):
                        dX[h % 2].inc(gp.dma_start(
                            out=xres[h % 2][:, q * (NT // 4):(q + 1) * (NT // 4), 0:128],
                            in_=src))

            exp_abs = {}
            pe_head_end = []
            gks = 0

            xt_ready = {}
            prev_yev_end = 0
            exp_abs_g = {}
            ggs = 0

            for h in range(H):
                pe_mark_h = sp.n
                act_mark_h = sa.n
                dve_mark_h = sv.n
                gtr0_h = len(tr_pe_pos)

                hp = head_pairs[h]
                qs = {}
                for idx, p_ in enumerate(hp):
                    if len(hp) <= max(NQC - 2, 1):
                        qc_ = min(2, NQC - len(hp)) + idx
                    else:
                        qc_ = idx // 2
                    qs[p_] = min(qc_, NQC - 1)
                slot_chunks = {qc: [] for qc in range(NQC)}
                for p_ in hp:
                    slot_chunks[qs[p_]].append(p_)
                pairs_by_region = {qc: [] for qc in range(NQC + 1)}
                for p_ in hp:
                    pairs_by_region[min(qs[p_] + 1, NQC)].append(p_)
                region_jobs = {qc: [] for qc in range(NQC + 1)}
                for p_ in hp:
                    reg = min(qs[p_] + 2, NQC)
                    for i in (2 * p_, 2 * p_ + 1):
                        nj = t_tiles[i][2] // 128
                        for j in range(0, nj, 4):
                            region_jobs[reg].append(("t", i, j, min(4, nj - j)))
                jobs_x = ([("x", h + 1, j, min(4, NT - j))
                           for j in range(0, NT, 4)] if h + 1 < H else [])
                wts = [5 * (4 * qc + 4) - 6 for qc in range(NQC)]
                tot = sum(wts)
                acc = 0
                taken = 0
                for qc in range(NQC):
                    acc += wts[qc]
                    upto = (len(jobs_x) * acc) // tot
                    region_jobs[qc] = jobs_x[taken:upto] + region_jobs[qc]
                    taken = upto
                prologue_jobs = ([("x", 0, j, min(4, NT - j))
                                  for j in range(0, NT, 4)] if h == 0 else [])

                qc_groups = []
                for qc in range(NQC):
                    gl = []
                    if qc >= 1:
                        gl += [[2 * i, 2 * i + 1] for i in range(2 * qc)]
                    gl += [[4 * qc + t0] for t0 in range(4)]
                    qc_groups.append(gl)

                pos = 0
                for j_ in prologue_jobs:
                    pos += 1
                    tr_pe_pos.append(pe_mark_h + pos)
                qc_plan = []
                for qc in range(NQC):
                    nk = 4 * qc + 4
                    groups = qc_groups[qc]
                    ng = len(groups)
                    pv_lists = [list(range(max(0, kt - 4 * qc), 4)) for kt in range(nk)]
                    jobs = region_jobs[qc]
                    share = [0] * ng
                    for x in range(len(jobs)):
                        share[(x * ng) // len(jobs)] += 1
                    seq = [("s", 0), ("s", 1)]
                    ji = 0
                    for gi in range(ng):
                        seq.append(("pv", gi))
                        for _ in range(share[gi]):
                            seq.append(("tr", jobs[ji])); ji += 1
                        if gi + 2 < ng:
                            seq.append(("s", gi + 2))
                    assert ji == len(jobs)
                    s_end = [0] * ng
                    pv_end_g = [0] * ng
                    for op in seq:
                        if op[0] == "s":
                            pos += 1
                            s_end[op[1]] = pos
                        elif op[0] == "pv":
                            group = groups[op[1]]
                            if len(group) == 1 and (group[0] - 4 * qc) % 2 == 1:
                                pos += 1
                            pv_end_g[op[1]] = pos
                        else:
                            pos += 1
                            tr_pe_pos.append(pe_mark_h + pos)
                            kind, a1, a2, n_ = op[1]
                            if kind == "t" and a2 + n_ == t_tiles[a1][2] // 128:
                                t_pe_end[a1] = pe_mark_h + pos
                    qc_plan.append((groups, seq, s_end, pv_end_g, pv_lists))
                    pos = pos
                for op1 in region_jobs[NQC]:
                    pos += 1
                    tr_pe_pos.append(pe_mark_h + pos)
                    kind, a1, a2, n_ = op1
                    if kind == "t" and a2 + n_ == t_tiles[a1][2] // 128:
                        t_pe_end[a1] = pe_mark_h + pos

                a = 0
                for j_ in prologue_jobs:
                    a += 1
                    tr_evict_pos.append(act_mark_h + a)
                if h == 0 and prologue_jobs:
                    xt_ready[0] = act_mark_h + a
                ks = gks
                gg = ggs
                for qc in range(NQC):
                    groups = qc_groups[qc]
                    for op in qc_plan[qc][1]:
                        if op[0] == "s":
                            gi = op[1]
                            group = groups[gi]
                            a += 1
                            for kt in group:
                                exp_abs[ks + kt] = act_mark_h + a
                            exp_abs_g[gg + gi] = act_mark_h + a
                        elif op[0] == "tr":
                            a += 1
                            tr_evict_pos.append(act_mark_h + a)
                            kind, a1, a2, n_ = op[1]
                            if kind == "x" and a2 + n_ == NT:
                                xt_ready[a1] = act_mark_h + a
                            if kind == "t" and a2 + n_ == t_tiles[a1][2] // 128:
                                t_ev_end[a1] = act_mark_h + a
                    gg += len(groups)
                    ks += 4 * qc + 4
                for op1 in region_jobs[NQC]:
                    a += 1
                    tr_evict_pos.append(act_mark_h + a)
                    kind, a1, a2, n_ = op1
                    if kind == "x" and a2 + n_ == NT:
                        xt_ready[a1] = act_mark_h + a
                    if kind == "t" and a2 + n_ == t_tiles[a1][2] // 128:
                        t_ev_end[a1] = act_mark_h + a
                n_groups_h = gg - ggs

                dpos = dve_mark_h
                evict_pos = {}
                for qc in range(NQC):
                    for j in range(2):
                        dpos += 1
                        evict_pos[(qc, j)] = dpos
                    for p_ in slot_chunks[qc]:
                        for k in range(9):
                            for i in (2 * p_, 2 * p_ + 1):
                                dpos += 1
                                if k == 0:
                                    t_copy_pos[i] = dpos
                                if k == 8:
                                    t_done_pos[i] = dpos

                @block.tensor
                def _(t, h=h, qc_plan=qc_plan, pe_mark_h=pe_mark_h, gks=gks,
                      ggs=ggs, region_jobs=region_jobs,
                      prologue_jobs=prologue_jobs, gtr0_h=gtr0_h,
                      prev_yev_end=prev_yev_end, evict_pos=evict_pos):
                    if h == 0:
                        barrier_waits(t, "tensor")
                        t.wait_ge(dX[0].h, 16 * 4)
                    g = gtr0_h
                    seen_x = set()
                    seen_t = set()

                    def emit_tr(op1, g):
                        kind, a1, a2, n_ = op1
                        if kind == "x":
                            if a1 not in seen_x:
                                t.wait_ge(dX[a1 % 2].h, 16 * 4 * (a1 // 2 + 1))
                                seen_x.add(a1)
                        else:
                            if a1 not in seen_t:
                                t.wait_ge(sv.h, t_done_pos[a1])
                                seen_t.add(a1)
                        if g >= 2:
                            t.wait_ge(sa.h, tr_evict_pos[g - 2])
                        for k in range(n_):
                            if kind == "x":
                                src = xres[a1 % 2][:, a2 + k, 0:128]
                            else:
                                src = tqt[a1 % 4][:, (a2 + k) * 128:(a2 + k + 1) * 128]
                            ins = t.transpose(ps_tr[g % 2][:, k * 128:(k + 1) * 128],
                                              src, id_bf[:])
                            if k == n_ - 1:
                                sp.inc(ins)
                        assert sp.n == tr_pe_pos[g]

                    for op1 in prologue_jobs:
                        emit_tr(op1, g); g += 1
                    ks = gks
                    gg = ggs
                    for qc in range(NQC):
                        groups, seq, s_end, pv_end_g, pv_lists = qc_plan[qc]

                        def emit_s(gi, ks=ks, qc=qc, gg=gg, groups=groups):
                            gx = gg + gi
                            if gx >= 2:
                                t.wait_ge(sa.h, exp_abs_g[gx - 2])
                            if gi == 0 and qc == 0:
                                t.wait_ge(sa.h, xt_ready[h])
                            diag = len(groups[gi]) == 1 and groups[gi][0] >= 4 * qc
                            for idx, kt in enumerate(groups[gi]):
                                qlo = 128 * max(0, kt - 4 * qc)
                                ins = t.matmul(
                                    ps_s[gx % 2][:, 512 * idx + qlo: 512 * idx + 512],
                                    xT[h % 2][:, kt * 128:(kt + 1) * 128],
                                    xT[h % 2][:, qc * 512 + qlo: qc * 512 + 512],
                                    start=True, stop=not diag)
                                if diag:
                                    ins = t.matmul(
                                        ps_s[gx % 2][:, qlo:qlo + 128],
                                        id_bf[:], mneg[:],
                                        start=False, stop=True)
                                if idx == len(groups[gi]) - 1:
                                    sp.inc(ins)

                        def emit_pv(gi, ks=ks, qc=qc, gg=gg, groups=groups,
                                    pv_lists=pv_lists):
                            gx = gg + gi
                            first = True
                            for idx, kt in enumerate(groups[gi]):
                                kk = ks + kt
                                for u in pv_lists[kt]:
                                    if first:
                                        t.wait_ge(sa.h, exp_abs[kk])
                                        if kt == 0 and qc >= 1:
                                            t.wait_ge(sv.h, evict_pos[(qc - 1, 1)])
                                        if kt == 0 and qc == 0 and prev_yev_end > 0:
                                            t.wait_ge(sv.h, prev_yev_end)
                                        first = False
                                    stop_ = (u % 2 == 1 and kt == 4 * qc + u)
                                    ins = t.matmul(
                                        ps_y[u // 2][:, (u % 2) * 129:(u % 2) * 129 + 129],
                                        expS[gx % 2][:, 512 * idx + u * 128: 512 * idx + (u + 1) * 128],
                                        xres[h % 2][:, kt, 0:129],
                                        start=(kt == 0 and u % 2 == 0),
                                        stop=stop_)
                                    if stop_:
                                        sp.inc(ins)

                        for op in seq:
                            if op[0] == "s":
                                emit_s(op[1])
                            elif op[0] == "pv":
                                emit_pv(op[1])
                            else:
                                emit_tr(op[1], g); g += 1
                        ks += 4 * qc + 4
                        gg += len(groups)
                    for op1 in region_jobs[NQC]:
                        emit_tr(op1, g); g += 1

                @block.scalar
                def _(s, h=h, qc_plan=qc_plan, pe_mark_h=pe_mark_h, gks=gks,
                      ggs=ggs, region_jobs=region_jobs,
                      prologue_jobs=prologue_jobs, gtr0_h=gtr0_h):
                    if h == 0:
                        barrier_waits(s, "scalar")
                    g = gtr0_h
                    seen_t = set()

                    def emit_cp(op1, g):
                        kind, a1, a2, n_ = op1
                        s.wait_ge(sp.h, tr_pe_pos[g])
                        if kind == "x":
                            sa.inc(s.copy(xT[a1 % 2][:, a2 * 128:(a2 + n_) * 128],
                                          ps_tr[g % 2][:, 0:n_ * 128]))
                        else:
                            if a1 not in seen_t and a1 >= 2:
                                s.wait_ge(dSCR[a1 % 2].h, 16 * ((a1 - 2) // 2 + 1))
                            seen_t.add(a1)
                            sa.inc(s.copy(tst[a1 % 2][:, a2:a2 + n_, :],
                                          ps_tr[g % 2][:, 0:n_ * 128]))
                        assert sa.n == tr_evict_pos[g]

                    for op1 in prologue_jobs:
                        emit_cp(op1, g); g += 1
                    ks = gks
                    gg = ggs
                    for qc in range(NQC):
                        groups, seq, s_end, pv_end_g, pv_lists = qc_plan[qc]
                        for op in seq:
                            if op[0] == "s":
                                gi = op[1]
                                group = groups[gi]
                                gx = gg + gi
                                s.wait_ge(sp.h, pe_mark_h + s_end[gi])
                                if len(group) == 2:
                                    sa.inc(s.activation(expS[gx % 2][:, 0:1024],
                                                        ps_s[gx % 2][:, 0:1024],
                                                        AF.Exp, bias=nbias[:],
                                                        scale=EXP_SCALE))
                                else:
                                    kt = group[0]
                                    qlo = 128 * max(0, kt - 4 * qc)
                                    sa.inc(s.activation(expS[gx % 2][:, qlo:512],
                                                        ps_s[gx % 2][:, qlo:512],
                                                        AF.Exp, bias=nbias[:],
                                                        scale=EXP_SCALE))
                                assert sa.n == exp_abs_g[gx]
                            elif op[0] == "tr":
                                emit_cp(op[1], g); g += 1
                        ks += 4 * qc + 4
                        gg += len(groups)
                    for op1 in region_jobs[NQC]:
                        emit_cp(op1, g); g += 1

                @block.vector
                def _(v, h=h, qc_plan=qc_plan, pe_mark_h=pe_mark_h,
                      evict_pos=evict_pos, slot_chunks=slot_chunks):
                    if h == 0:
                        barrier_waits(v, "vector")
                        v.wait_ge(sg.h, INIT_N)
                    for qc in range(NQC):
                        groups, seq, s_end, pv_end_g, pv_lists = qc_plan[qc]
                        for j in range(2):
                            gi_last = groups.index([4 * qc + 2 * j + 1])
                            v.wait_ge(sp.h, pe_mark_h + pv_end_g[gi_last])
                            if j == 0:
                                idx_par = (NQC // 2) * h + qc // 2
                                if idx_par >= 1:
                                    v.wait_ge(dYS[qc % 2].h, 16 * idx_par)
                            sv.inc(v.tensor_copy(
                                ystg[qc % 2][:, 2 * j:2 * j + 2, :],
                                ps_y[j].rearrange("p (a e) -> p a e", e=129)))
                            assert sv.n == evict_pos[(qc, j)]
                        for p_ in slot_chunks[qc]:
                            emit_t_chunk(v, p_)

                @block.gpsimd
                def _(gp, h=h, gks=gks, evict_pos=evict_pos):
                    for qc in range(NQC):
                        gp.wait_ge(sv.h, evict_pos[(qc, 1)])
                        dYS[qc % 2].inc(gp.dma_start(
                            out=ys[h].rearrange("(a p) e -> p a e", p=128)
                            [:, 4 * qc:4 * qc + 4, :],
                            in_=ystg[qc % 2][:, :, :]))

                @block.sync
                def _(sy, h=h):
                    if h == 0:
                        barrier_waits(sy, "sync")
                    hp_ = head_pairs[h]
                    for k2, p_ in enumerate(hp_):
                        emit_t_sync_loads(sy, p_)
                        if k2 >= 1:
                            emit_t_scratch(sy, hp_[k2 - 1])
                    if hp_:
                        emit_t_scratch(sy, hp_[-1])

                gks += sum(4 * qc + 4 for qc in range(NQC))
                ggs += n_groups_h
                pe_head_end.append(sp.n)
                prev_yev_end = evict_pos[(NQC - 1, 1)]

                if h + 2 < H:
                    @block.gpsimd
                    def _(gp, h=h):
                        gp.wait_ge(sp.h, pe_head_end[h])
                        for q, src in enumerate(head_x_src(h + 2)):
                            dX[h % 2].inc(gp.dma_start(
                                out=xres[h % 2][:, q * (NT // 4):(q + 1) * (NT // 4), 0:128],
                                in_=src))

            snap_marks()

        s2 = ExitStack()
        with s2:
            e2 = s2.enter_context
            xb = [e2(nc.sbuf_tensor(f"xb{i}", [128, 4, D], F32)) for i in range(2)]
            yb = e2(nc.sbuf_tensor("yb", [128, 2, H, 129], F32))
            rcp16 = e2(nc.sbuf_tensor("rcp16", [128, H], F32))
            X1T = [e2(nc.sbuf_tensor(f"X1T{i}", [128, NDT, 512], BF16)) for i in range(1)]
            hbuf = e2(nc.sbuf_tensor("hbuf", [128, NHC, 512], BF16))
            W1K = min(5, HID // 128)
            w1b = [e2(nc.sbuf_tensor(f"w1b{i}", [128, D], BF16))
                   for i in range(W1K)]
            w2b = [e2(nc.sbuf_tensor(f"w2b{i}", [128, HID // 4], BF16)) for i in range(2)]
            trel = [e2(nc.sbuf_tensor(f"trel{i}", [128, 512], BF16)) for i in range(8)]
            hsq = trel
            lnt = e2(nc.sbuf_tensor("lnt", [1, 512], F32))
            rsb4 = e2(nc.sbuf_tensor("rsb4", [128, 4], F32))
            yTd = [e2(nc.sbuf_tensor(f"yTd{i}", [128, 512], F32)) for i in range(2)]
            sqb = e2(nc.sbuf_tensor("sqb", [128, D // 4], F32))
            statv = e2(nc.sbuf_tensor("statv", [128, 8], F32))
            s1c = statv[:, 0:1]
            s2c = statv[:, 1:2]
            mc = statv[:, 2:3]
            m2c = statv[:, 3:4]
            vc = statv[:, 4:5]
            rsc = statv[:, 5:6]
            ps_h = [e2(nc.psum_tensor(f"ps_h{i}", [128, 512], F32)) for i in range(2)]
            ps_m = [e2(nc.psum_tensor(f"ps_m{i}", [128, 512], F32)) for i in range(2)]
            ps_den = e2(nc.psum_tensor("ps_den", [128, 512], F32))
            ps_bc = e2(nc.psum_tensor("ps_bc", [128, 512], F32))
            ps_tf = [e2(nc.psum_tensor(f"ps_tf{i}", [128, 128], F32)) for i in range(2)]

            trp_pe = []
            trp_ev = []
            LN_OPS = 17
            CHUNK_OPS = H + 1 + LN_OPS

            def emit_ln(v, st, xap, gate_apply):
                Q4 = D // 4
                st(v.tensor_reduce(s1c[:], xap, AX.X, ALU.add))
                st(v.tensor_scalar(mc[:], s1c[:], 1.0 / D, None, ALU.mult))
                for q4 in range(4):
                    xs = xap[:, q4 * Q4:(q4 + 1) * Q4]
                    st(v.tensor_tensor(sqb[:, :], xs, xs, ALU.mult))
                    st(v.tensor_reduce(m2c[:] if q4 else s2c[:], sqb[:, :],
                                       AX.X, ALU.add))
                    if q4:
                        st(v.tensor_add(s2c[:], s2c[:], m2c[:]))
                st(v.tensor_scalar(s2c[:], s2c[:], 1.0 / D, None, ALU.mult))
                st(v.tensor_tensor(m2c[:], mc[:], mc[:], ALU.mult))
                st(v.tensor_sub(vc[:], s2c[:], m2c[:]))
                gate_apply()
                st(v.tensor_scalar(xap, xap, mc[:], rsc[:],
                                   ALU.subtract, ALU.mult))

            pe_b, sa_b, sv_b, dxy_b = sp.n, sa.n, sv.n, dXY.n

            def bp0(tp, j, sv_b=sv_b):
                return sv_b + tp * CHUNK_OPS + j

            @block.sync
            def _(sy):
                barrier_waits(sy, "sync")
                sy.wait_ge(dYS[0].h, 16 * (NQC // 2) * H)
                sy.wait_ge(dYS[1].h, 16 * (NQC // 2) * H)
                sy.wait_ge(dSCR[0].h, 16 * ((NTT + 1) // 2))
                sy.wait_ge(dSCR[1].h, 16 * (NTT // 2))
                dXY.inc(sy.dma_start(
                    out=xb[0][:, :, :],
                    in_=x_hbm[0:512, :].rearrange("(a p) d -> p a d", p=128)))
                for tp in range(4):
                    if tp == 2:
                        sy.wait_ge(sv.h, bp0(1, H + 1))
                    for h in range(H):
                        dXY.inc(sy.dma_start(
                            out=yb[:, tp % 2, h, :],
                            in_=ys[h].rearrange("(a p) e -> p a e", p=128)[:, tp, :]))

            @block.vector
            def _(v):
                def st(ins):
                    sv.inc(ins)
                    v.wait_ge(sv.h, sv.n)
                barrier_waits(v, "vector")
                for tp in range(4):
                    v.wait_ge(dXY.h, dxy_b + 16 * (1 + (2 if tp < 2 else 4) * H))
                    st(v.reciprocal(rcp16[:, :], yb[:, tp % 2, :, 128]))
                    for h in range(H):
                        st(v.scalar_tensor_tensor(
                            xb[0][:, tp, h * 128:(h + 1) * 128],
                            yb[:, tp % 2, h, 0:128],
                            rcp16[:, h:h + 1],
                            xb[0][:, tp, h * 128:(h + 1) * 128],
                            ALU.mult, ALU.add))
                    emit_ln(v, st, xb[0][:, tp, :],
                            lambda tp=tp: v.wait_ge(sa.h, sa_b + 2 * tp + 2))

            b0_stt_end = [bp0(tp, H + 1) for tp in range(4)]

            @block.scalar
            def _(s):
                def st(ins):
                    sa.inc(ins)
                    s.wait_ge(sa.h, sa.n)
                barrier_waits(s, "scalar")
                for tp in range(4):
                    s.wait_ge(sv.h, bp0(tp, H + LN_OPS))
                    st(s.activation(vc[:], vc[:], AF.Ln, bias=eps_ln[:]))
                    st(s.activation(rsc[:], vc[:], AF.Exp, scale=-0.5))

            pe_bT = sp.n
            for g in range(4 * NDT):
                trp_pe.append(pe_bT + g + 1)
                trp_ev.append(("sa", sa.n + g + 1))

            @block.tensor
            def _(t):
                barrier_waits(t, "tensor")
                g = 0
                for tq in range(4):
                    for dt in range(NDT):
                        if dt == 0:
                            t.wait_ge(sv.h, bp0(tq, H + 1 + LN_OPS))
                        if g >= 2:
                            k_, n_ = trp_ev[g - 2]
                            t.wait_ge(sa.h if k_ == "sa" else sv.h, n_)
                        sp.inc(t.transpose(ps_tf[g % 2][:],
                                           xb[0][:, tq, dt * 128:(dt + 1) * 128],
                                           id_f[:]))
                        assert sp.n == trp_pe[g]
                        g += 1

            @block.scalar
            def _(s):
                g = 0
                for tq in range(4):
                    for dt in range(NDT):
                        s.wait_ge(sp.h, trp_pe[g])
                        sa.inc(s.copy(X1T[0][:, dt, tq * 128:(tq + 1) * 128],
                                      ps_tf[g % 2][:]))
                        assert ("sa", sa.n) == trp_ev[g]
                        g += 1

            gtb = 4 * NDT
            x1t_ready = sa.n
            prev = {"b_stt_end3": b0_stt_end[3]}
            snap_marks()

            for c in range(NQC):
                pe_c, sa_c, sv_c, dxy_c = sp.n, sa.n, sv.n, dXY2.n
                dw1_c = [c_.n for c_ in dW1]
                dw2_c = [c_.n for c_ in dW2]
                has_b = c + 1 < NQC
                nxt = (c + 1) % 2

                ch_ins = ({6: 0, 20: 1, 34: 2, 44: 3} if NHC == 64 else
                          {(2 * tp + 1) * NHC // 8: tp for tp in range(4)}) \
                    if has_b else {}
                bt_ins = {}
                if has_b:
                    for i_ in range(4):
                        bt_ins.setdefault(max(0, (i_ + 1) * NDT // 4 - 1), []).append(i_)

                p = 0
                m1_end = {}
                den_pos = {}
                m2_pos = {}
                m2_end = {}
                ctr_pe = {}
                gt0_c = gtb
                gtw = gtb
                bt_pe = {}
                DLAG = 8
                for hc in range(NHC):
                    p += 1; m1_end[hc] = p
                    if hc >= DLAG:
                        p += 1; den_pos[hc - DLAG] = p
                for dn in range(NHC - DLAG, NHC):
                    p += 1; den_pos[dn] = p
                p += 1; bcast_pos = p
                m2_q = {}
                m2_jobs = {}
                for dt in range(NDT):
                    jl = []
                    if dt >= 2:
                        jl += [("c", dt - 2, tq) for tq in range(4)]
                    for i_ in bt_ins.get(dt, []):
                        jl += [("b", NDT * i_ + j_, 0) for j_ in range(NDT)]
                    if dt == NDT - 1:
                        jl += [("c", NDT - 2, tq) for tq in range(4)]
                    m2_jobs[dt] = jl
                m2_seq = {}
                for dt in range(NDT):
                    jl = m2_jobs[dt]
                    span = max(NHC - 4, 1)
                    ins_after = [0] * NHC
                    for x in range(len(jl)):
                        ins_after[min((x * span) // max(len(jl), 1), NHC - 5)] += 1
                    seq = []
                    ji = 0
                    for hc in range(NHC):
                        seq.append(("mm", hc))
                        for _ in range(ins_after[hc]):
                            seq.append(("job", jl[ji])); ji += 1
                    assert ji == len(jl)
                    if dt == NDT - 1:
                        seq += [("job", ("c", NDT - 1, tq)) for tq in range(4)]
                    m2_seq[dt] = seq
                    m2_pos[dt] = p
                    qn = 0
                    for op in seq:
                        if op[0] == "mm":
                            if (op[1] + 1) % (NHC // 4) == 0:
                                p += 1
                                m2_q[(dt, qn)] = p
                                qn += 1
                                if qn == 4:
                                    m2_end[dt] = p
                        else:
                            kindc, a1, a2 = op[1]
                            p += 1
                            trp_pe.append(pe_c + p)
                            if kindc == "c":
                                ctr_pe[(a1, a2)] = gtw
                            else:
                                bt_pe[a1] = gtw
                            gtw += 1
                pe_total_c = p

                d = 0
                d_tt = {}
                d_hsq = {}
                b_var = {}
                b_apply = {}
                b_stt_end = {}
                for hc in range(NHC):
                    d += 1; d_tt[hc] = sv_c + d
                    d += 1; d_hsq[hc] = sv_c + d
                    if hc in ch_ins:
                        tp = ch_ins[hc]
                        b_stt_end[tp] = sv_c + d + 1 + H
                        b_var[tp] = sv_c + d + H + LN_OPS
                        b_apply[tp] = sv_c + d + CHUNK_OPS
                        d += CHUNK_OPS
                d += 1; d_bcev = sv_c + d
                d_ev = {}
                d_add = {}
                for dt in range(NDT):
                    d += 1; d_ev[dt] = sv_c + d
                    for tq in range(4):
                        d += 1; d_add[(dt, tq)] = sv_c + d
                d_ln2 = {}
                for tq in range(4):
                    for j in range(1, LN_OPS + 1):
                        d += 1; d_ln2[(tq, j)] = sv_c + d

                a = 0
                a_relu = {}
                a_bln = {}
                for hc in range(NHC):
                    a += 1; a_relu[hc] = sa_c + a
                    if hc - 4 in ch_ins:
                        a += 2; a_bln[ch_ins[hc - 4]] = sa_c + a
                a += 2; a_rms = sa_c + a
                a_xev = {}
                if has_b:
                    for g in range(4 * NDT):
                        a += 1; a_xev[g] = sa_c + a
                a_cln = {}
                for tq in range(4):
                    a += 2; a_cln[tq] = sa_c + a
                tmp_ev = {}
                for (dtc, tq), gti in ctr_pe.items():
                    tmp_ev[gti] = ("sv", d_add[(dtc, tq)])
                for jobi, gti in bt_pe.items():
                    tmp_ev[gti] = ("sa", a_xev[jobi])
                while len(trp_ev) < gtw:
                    trp_ev.append(None)
                for gti in range(gt0_c, gtw):
                    trp_ev[gti] = tmp_ev[gti]
                gtb = gtw

                @block.gpsimd
                def _(gp, c=c, has_b=has_b, nxt=nxt, dxy_c=dxy_c, prev=prev,
                      d_ln2=d_ln2, b_stt_end=b_stt_end):
                    if has_b:
                        if c >= 1:
                            gp.wait_ge(dOUT.h, 16 * c)
                        gp.wait_ge(sv.h, prev["b_stt_end3"])
                        dXY2.inc(gp.dma_start(
                            out=xb[nxt][:, :, :],
                            in_=x_hbm[(c + 1) * 512:(c + 2) * 512, :]
                            .rearrange("(a p) d -> p a d", p=128)))
                        for tp in range(2):
                            for h in range(H):
                                dXY2.inc(gp.dma_start(
                                    out=yb[:, tp % 2, h, :],
                                    in_=ys[h].rearrange("(a p) e -> p a e", p=128)
                                    [:, 4 * (c + 1) + tp, :]))
                        gp.wait_ge(sv.h, b_stt_end[1])
                        for tp in range(2, 4):
                            for h in range(H):
                                dXY2.inc(gp.dma_start(
                                    out=yb[:, tp % 2, h, :],
                                    in_=ys[h].rearrange("(a p) e -> p a e", p=128)
                                    [:, 4 * (c + 1) + tp, :]))
                    gp.wait_ge(sv.h, d_ln2[(3, LN_OPS)])
                    dOUT.inc(gp.dma_start(
                        out=out_hbm[c * 512:(c + 1) * 512, :]
                        .rearrange("(a p) d -> p a d", p=128),
                        in_=xb[c % 2][:, :, :]))

                @block.sync
                def _(sy, c=c, prev=prev, has_b=has_b, m1_end=m1_end,
                      m2_q=m2_q):
                    if c == 0:
                        barrier_waits(sy, "sync")
                        sy.wait_ge(dSCR[0].h, 16 * ((NTT + 1) // 2))
                        sy.wait_ge(dSCR[1].h, 16 * (NTT // 2))
                    items = []
                    if c == 0:
                        for j in range(W1K):
                            items.append((-1, 0, j))
                    for j in range(c * NHC + W1K, (c + 1) * NHC + W1K):
                        if j >= (c + 1) * NHC and not has_b:
                            break
                        items.append((pe_c + m1_end[j - W1K - c * NHC], 0, j))
                    for dt in range(NDT):
                        for q_ in range(4):
                            if q_ >= 2:
                                g_ = pe_c + m2_q[(dt, q_ - 2)]
                            elif dt >= 1:
                                g_ = pe_c + m2_q[(dt - 1, min(q_ + 2, 3))]
                            elif c >= 1:
                                g_ = prev["m2_q15"][q_]
                            else:
                                g_ = 0
                            items.append((g_, 1, (dt, q_)))
                    for g_, kind_, a_ in sorted(items, key=lambda x: (x[0], x[1] == 0)):
                        if g_ > 0:
                            sy.wait_ge(sp.h, g_)
                        if kind_ == 0:
                            j = a_
                            dW1[j % W1K].inc(sy.dma_start(
                                out=w1b[j % W1K][:, :], in_=w1t[j % NHC, :, :]))
                        else:
                            dt, q_ = a_
                            dW2[q_].inc(sy.dma_start(
                                out=w2b[q_ % 2][:, :],
                                in_=w2t[dt, :, q_ * (HID // 4):(q_ + 1) * (HID // 4)]))

                @block.tensor
                def _(t, c=c, has_b=has_b, nxt=nxt, prev=prev, dw1_c=dw1_c,
                      dw2_c=dw2_c, m1_end=m1_end, den_pos=den_pos,
                      bcast_pos=bcast_pos, m2_pos=m2_pos, m2_end=m2_end,
                      d_hsq=d_hsq, d_ev=d_ev, b_apply=b_apply, a_relu=a_relu,
                      a_rms=a_rms, ch_ins=ch_ins, bt_ins=bt_ins,
                      ctr_pe=ctr_pe, bt_pe=bt_pe, gt0_c=gt0_c,
                      m2_seq=m2_seq):
                    if c == 0:
                        barrier_waits(t, "tensor")
                        t.wait_ge(sa.h, x1t_ready)
                    else:
                        t.wait_ge(sa.h, prev["x1t_ready"])
                    seen_bq = set()
                    for hc in range(NHC):
                        jg = c * NHC + hc
                        t.wait_ge(dW1[jg % W1K].h, 16 * (jg // W1K + 1))
                        if hc >= 2:
                            t.wait_ge(sa.h, a_relu[hc - 2])
                        elif c >= 1:
                            t.wait_ge(sa.h, prev["a_relu"][NHC - 2 + hc])
                        for dt in range(NDT):
                            ins = t.matmul(ps_h[hc % 2][:, :],
                                           w1b[(c * NHC + hc) % W1K][:, dt * 128:(dt + 1) * 128],
                                           X1T[0][:, dt, :],
                                           start=(dt == 0), stop=(dt == NDT - 1))
                            if dt == NDT - 1:
                                sp.inc(ins)
                        if hc >= 8:
                            dn = hc - 8
                            t.wait_ge(sv.h, d_hsq[dn])
                            if dn == 0 and c >= 1:
                                t.wait_ge(sa.h, prev["a_rms"])
                            sp.inc(t.matmul(ps_den[0:1, :], ones_bf[:],
                                            hsq[dn % 8][:, :],
                                            start=(dn == 0), stop=False))
                        assert sp.n == pe_c + m1_end[hc] + (1 if hc >= 8 else 0)
                    for dn in range(NHC - 8, NHC):
                        t.wait_ge(sv.h, d_hsq[dn])
                        sp.inc(t.matmul(ps_den[0:1, :], ones_bf[:],
                                        hsq[dn % 8][:, :],
                                        start=False, stop=(dn == NHC - 1)))
                    t.wait_ge(sa.h, a_rms)
                    if c >= 1:
                        t.wait_ge(sv.h, prev["d_bcev"])
                    for tq in range(4):
                        ins = t.matmul(ps_bc[:, tq:tq + 1],
                                       lnt[0:1, tq * 128:(tq + 1) * 128],
                                       ones_f[0:1, 0:1],
                                       start=(tq == 0), stop=(tq == 3))
                        if tq == 3:
                            sp.inc(ins)
                    assert sp.n == pe_c + bcast_pos
                    seen_ct = set()
                    for dt in range(NDT):
                        for op in m2_seq[dt]:
                            if op[0] == "mm":
                                hc = op[1]
                                q_ = hc // (NHC // 4)
                                off = hc - q_ * (NHC // 4)
                                if off == 0:
                                    t.wait_ge(dW2[q_].h, dw2_c[q_] + 16 * (dt + 1))
                                if hc == 0:
                                    if dt >= 2:
                                        t.wait_ge(sv.h, d_ev[dt - 2])
                                    elif c >= 1:
                                        t.wait_ge(sv.h, prev["d_ev"][NDT - 2 + dt])
                                ins = t.matmul(
                                    ps_m[dt % 2][:, :],
                                    w2b[q_ % 2][:, off * 128:(off + 1) * 128],
                                    hbuf[:, hc, :],
                                    start=(hc == 0), stop=(hc == NHC - 1))
                                if (hc + 1) % (NHC // 4) == 0:
                                    sp.inc(ins)
                            else:
                                kindc, a1, a2 = op[1]
                                if kindc == "c":
                                    gti = ctr_pe[(a1, a2)]
                                    if a1 not in seen_ct:
                                        t.wait_ge(sv.h, d_ev[a1])
                                        seen_ct.add(a1)
                                    src = yTd[a1 % 2][:, a2 * 128:(a2 + 1) * 128]
                                else:
                                    gti = bt_pe[a1]
                                    tq, dtp = a1 // NDT, a1 % NDT
                                    if tq not in seen_bq:
                                        t.wait_ge(sv.h, b_apply[tq])
                                        seen_bq.add(tq)
                                    src = xb[nxt][:, tq, dtp * 128:(dtp + 1) * 128]
                                if gti >= 2:
                                    k_, n_ = trp_ev[gti - 2]
                                    t.wait_ge(sa.h if k_ == "sa" else sv.h, n_)
                                sp.inc(t.transpose(ps_tf[gti % 2][:], src, id_f[:]))
                                assert sp.n == trp_pe[gti]

                @block.scalar
                def _(s, c=c, has_b=has_b, nxt=nxt, m1_end=m1_end,
                      den_pos=den_pos, d_tt=d_tt, b_var=b_var, ch_ins=ch_ins,
                      d_ln2=d_ln2, bt_pe=bt_pe, a_xev=a_xev, prev=prev):
                    def st(ins):
                        sa.inc(ins)
                        s.wait_ge(sa.h, sa.n)

                    if c == 0:
                        barrier_waits(s, "scalar")
                    for hc in range(NHC):
                        s.wait_ge(sp.h, pe_c + m1_end[hc])
                        if hc >= 8:
                            s.wait_ge(sp.h, pe_c + den_pos[hc - 8])
                        elif c >= 1:
                            s.wait_ge(sp.h, prev["pe_c"] + prev["den_pos"][NHC - 8 + hc])
                        sa.inc(s.activation(trel[hc % 8][:, :], ps_h[hc % 2][:, :],
                                            AF.Relu))
                        if hc - 4 in ch_ins:
                            tp = ch_ins[hc - 4]
                            s.wait_ge(sv.h, b_var[tp])
                            st(s.activation(vc[:], vc[:], AF.Ln, bias=eps_ln[:]))
                            st(s.activation(rsc[:], vc[:], AF.Exp, scale=-0.5))
                    s.wait_ge(sp.h, pe_c + den_pos[NHC - 1])
                    st(s.activation(lnt[:, :], ps_den[0:1, :], AF.Ln,
                                    bias=eps_rms[0:1, :], scale=1.0 / HID))
                    st(s.activation(lnt[:, :], lnt[:, :], AF.Exp, scale=-0.5))
                    if has_b:
                        for jobi in range(4 * NDT):
                            tq, dtp = jobi // NDT, jobi % NDT
                            gti = bt_pe[jobi]
                            s.wait_ge(sp.h, trp_pe[gti])
                            sa.inc(s.copy(X1T[0][:, dtp, tq * 128:(tq + 1) * 128],
                                          ps_tf[gti % 2][:]))
                            assert ("sa", sa.n) == trp_ev[gti]
                    for tq in range(4):
                        s.wait_ge(sv.h, d_ln2[(tq, LN_OPS - 1)])
                        st(s.activation(vc[:], vc[:], AF.Ln, bias=eps_ln[:]))
                        st(s.activation(rsc[:], vc[:], AF.Exp, scale=-0.5))

                @block.vector
                def _(v, c=c, has_b=has_b, nxt=nxt, dxy_c=dxy_c, a_relu=a_relu,
                      den_pos=den_pos, bcast_pos=bcast_pos, m2_end=m2_end,
                      ctr_pe=ctr_pe, ch_ins=ch_ins, a_bln=a_bln, a_cln=a_cln,
                      prev=prev, d_ev=d_ev, d_add=d_add):
                    def st(ins):
                        sv.inc(ins)
                        v.wait_ge(sv.h, sv.n)
                    if c == 0:
                        barrier_waits(v, "vector")
                    for hc in range(NHC):
                        v.wait_ge(sa.h, a_relu[hc])
                        st(v.tensor_tensor(hbuf[:, hc, :], trel[hc % 8][:, :],
                                           trel[hc % 8][:, :], ALU.mult))
                        st(v.tensor_tensor(hsq[hc % 8][:, :], hbuf[:, hc, :],
                                           hbuf[:, hc, :], ALU.mult))
                        if hc in ch_ins:
                            tp = ch_ins[hc]
                            v.wait_ge(dXY2.h, dxy_c + 16 * (1 + (2 if tp < 2 else 4) * H))
                            st(v.reciprocal(rcp16[:, :], yb[:, tp % 2, :, 128]))
                            for h in range(H):
                                st(v.scalar_tensor_tensor(
                                    xb[nxt][:, tp, h * 128:(h + 1) * 128],
                                    yb[:, tp % 2, h, 0:128],
                                    rcp16[:, h:h + 1],
                                    xb[nxt][:, tp, h * 128:(h + 1) * 128],
                                    ALU.mult, ALU.add))
                            emit_ln(v, st, xb[nxt][:, tp, :],
                                    lambda tp=tp: v.wait_ge(sa.h, a_bln[tp]))
                            assert sv.n == b_apply[tp]
                    v.wait_ge(sp.h, pe_c + bcast_pos)
                    st(v.tensor_copy(rsb4[:, :], ps_bc[:, 0:4]))
                    assert sv.n == d_bcev
                    for dt in range(NDT):
                        v.wait_ge(sp.h, pe_c + m2_end[dt])
                        if dt >= 2:
                            v.wait_ge(sp.h, trp_pe[ctr_pe[(dt - 2, 3)]])
                        elif c >= 1:
                            v.wait_ge(sp.h, prev["ctr_last_pe"][dt])
                        st(v.tensor_copy(yTd[dt % 2][:, :], ps_m[dt % 2][:, :]))
                        assert sv.n == d_ev[dt]
                        for tq in range(4):
                            v.wait_ge(sp.h, trp_pe[ctr_pe[(dt, tq)]])
                            st(v.scalar_tensor_tensor(
                                xb[c % 2][:, tq, dt * 128:(dt + 1) * 128],
                                ps_tf[ctr_pe[(dt, tq)] % 2][:],
                                rsb4[:, tq:tq + 1],
                                xb[c % 2][:, tq, dt * 128:(dt + 1) * 128],
                                ALU.mult, ALU.add))
                    for tq in range(4):
                        emit_ln(v, st, xb[c % 2][:, tq, :],
                                lambda tq=tq: v.wait_ge(sa.h, a_cln[tq]))
                        assert sv.n == d_ln2[(tq, LN_OPS)]

                prev = {
                    "b_stt_end3": b_stt_end.get(3, prev.get("b_stt_end3", 0)),
                    "x1t_ready": (a_xev[4 * NDT - 1] if has_b else 0),
                    "a_relu": a_relu,
                    "a_rms": a_rms,
                    "d_bcev": d_bcev,
                    "d_ev": d_ev,
                    "pe_c": pe_c,
                    "den_pos": den_pos,
                    "ctr_last_pe": {dtx: trp_pe[ctr_pe[(NDT - 2 + dtx, 3)]]
                                    for dtx in range(2)},
                    "m2_q15": [pe_c + m2_q[(NDT - 1, min(q_ + 2, 3))]
                               for q_ in range(2)],
                }

        @block.gpsimd
        def _(gp):
            gp.wait_ge(dOUT.h, 16 * NQC)

    return nc


def kernel(L0_attn_x, L0_ffn_W_fc, L0_ffn_W_proj):
    x = np.ascontiguousarray(np.asarray(L0_attn_x, dtype=np.float32))
    wfc = np.ascontiguousarray(np.asarray(L0_ffn_W_fc, dtype=np.float32))
    wproj = np.ascontiguousarray(np.asarray(L0_ffn_W_proj, dtype=np.float32))
    B = x.shape[0]
    nc = build_program(L=x.shape[1], D=x.shape[2], HID=wfc.shape[0])
    in_maps = [{"x": x[b], "wfc": wfc, "wproj": wproj} for b in range(B)]
    res = run_bass_kernel_spmd(nc, in_maps, core_ids=list(range(B)))
    return np.stack([res.results[b]["out"] for b in range(B)], axis=0)
